# version 12 (final)
# speedup vs baseline: 1.4971x; 1.0123x over previous
import math
import os
import numpy as np
from contextlib import ExitStack

import concourse.bass as bass
import concourse.mybir as mybir
from concourse.bass_utils import run_bass_kernel_spmd

F32 = mybir.dt.float32
BF16 = mybir.dt.bfloat16
AF = mybir.ActivationFunctionType
ALU = mybir.AluOpType
AX = mybir.AxisListType.X

C = 1024
B = 2
D = 1024
NCORE = 8
T = C * NCORE
M0 = 256
NEG = -1e30

ENGINES = ("sync", "tensor", "vector", "scalar", "gpsimd")



SEM_ALIAS = {
    "ld0": "ds0", "lS0": "ds0", "lw0": "ds0", "lf0": "ds0", "lx0": "ds0",
    "ld1": "ds1", "lS1": "ds1", "lw1": "ds1", "lf1": "ds1", "lx1": "ds1",
    "lsf0": "ds2", "lg0": "ds2", "lP0": "ds2",
    "lsf1": "ds3", "lg1": "ds3", "lP1": "ds3",
    "lgs0": "ds4", "lgs1": "ds5",
}

SHARED_EI = {"xin"}
SHARED = {
    "xnfm", "vfm", "qfm", "ktm", "vtm", "chat", "ccol", "Lrow", "S", "wrows",
    "sfm", "gs", "x1tm", "xn1fm", "q1fm", "k1fm", "v1tm", "pptm", "Amat", "Bvec",
    "ret", "statp", "xmid0", "xmid1", "xint1", "xint2",
    "Sall", "Lall", "Aall", "Bvall", "statall", "halo0g", "halo1g",
    "halo0c", "halo1c",
} | SHARED_EI

ARENA_BYTES = 204 * 1024


class Prog:
    def __init__(self, name="prog", fused=False):
        self.nc = bass.Bass("TRN2", target_bir_lowering=False)
        self.es = ExitStack()
        self.q = {e: [] for e in ENGINES}
        self.semh = {}
        self.semc = {}
        self.name = name
        self.npsum = 0
        self.fused = fused
        self.prefix = ""
        self.shared_drams = {}
        self.ei_names = set()
        if fused:
            self.arena = self.es.enter_context(
                self.nc.sbuf_tensor("ARENA", [128, ARENA_BYTES], mybir.dt.uint8))
            self.acur = 0
            self.pool = [self.es.enter_context(
                self.nc.psum_tensor(f"PBANK{i}", [128, 512], F32)) for i in range(8)]
            self.pidx = 0

    def phase(self, pre):
        self.prefix = pre if self.fused else ""
        if self.fused:
            self.acur = 0
            self.pidx = 0
            gate = [(s, self.val(s)) for s in ("st", "cc") if s in self.semc]
            if gate:
                for e in ENGINES:
                    self.wait(e, *gate)

    GATHERED = {"Sall", "Lall", "Aall", "Bvall", "statall", "halo0g", "halo1g"}

    def dram(self, name, shape, dt, kind="Internal"):
        if self.fused and name in SHARED:
            if name in self.shared_drams:
                return self.shared_drams[name]
            k = "ExternalInput" if name in SHARED_EI else (
                "ExternalOutput" if name == "y" else "Internal")
            aspace = "Shared" if name in Prog.GATHERED else "Local"
            ap = self.nc.dram_tensor(name, list(shape), dt, kind=k, addr_space=aspace).ap()
            self.shared_drams[name] = ap
            if k == "ExternalInput":
                self.ei_names.add(name)
            return ap
        nm = self.prefix + name
        if self.fused and kind == "ExternalOutput" and name != "y":
            kind = "Internal"
        if kind == "ExternalInput":
            self.ei_names.add(nm)
        return self.nc.dram_tensor(nm, list(shape), dt, kind=kind).ap()

    def sbuf(self, name, shape, dt):
        if not self.fused:
            return self.es.enter_context(self.nc.sbuf_tensor(self.prefix + name, list(shape), dt))
        esz = {F32: 4, BF16: 2}[dt]
        nby = shape[1] * esz
        off = (self.acur + 63) // 64 * 64
        assert off + nby <= ARENA_BYTES, f"arena overflow at {self.prefix}{name}: {off}+{nby}"
        self.acur = off + nby
        view = self.arena.bitcast(dt)
        assert shape[0] <= 128
        return view[0:shape[0], off // esz:off // esz + shape[1]]

    def psum(self, name, shape=(128, 512), dt=F32):
        if not self.fused:
            self.npsum += 1
            assert self.npsum <= 8, "more than 8 PSUM banks"
            return self.es.enter_context(self.nc.psum_tensor(self.prefix + name, list(shape), dt))
        bank = self.pool[self.pidx]
        self.pidx += 1
        assert self.pidx <= 8, "psum pool exhausted in phase " + self.prefix
        if dt == F32:
            return bank[0:shape[0], 0:shape[1]]
        return bank[:].bitcast(dt)[0:shape[0], 0:shape[1]]

    def sem(self, name):
        name = SEM_ALIAS.get(name, name)
        if name not in self.semh:
            self.semh[name] = self.es.enter_context(self.nc.semaphore(name))
            self.semc[name] = 0
        return name

    def op(self, eng, meth, *args, wait=(), inc=None, incn=1, incs=(), **kw):
        tgt = None
        wait = [(SEM_ALIAS.get(s, s), v) for s, v in wait]
        incs = [(SEM_ALIAS.get(s, s), n) for s, n in incs]
        if inc is not None:
            incs.insert(0, (SEM_ALIAS.get(inc, inc), incn))
        for s, n in incs:
            self.sem(s)
            prev = self.semc[s]
            self.semc[s] += n
            if tgt is None:
                tgt = self.semc[s]
            if prev > 0 and s not in ("ld", "wldg", "wld") and all(w != s for w, _ in wait):
                wait.append((s, prev))
        for s, _ in wait:
            self.sem(s)
        wait = tuple(wait)
        incs = tuple(incs)

        def run(h, meth=meth, args=args, kw=kw, wait=wait, incs=incs):
            for s, v in wait:
                if v > 0:
                    h.wait_ge(self.semh[s], v)
            r = getattr(h, meth)(*args, **kw)
            for s, n in incs:
                r = r.then_inc(self.semh[s], n)
        run.meta = (eng, meth, wait, incs)
        self.q[eng].append(run)
        return tgt

    def dma(self, eng, out, in_, wait=(), inc=None, **kw):
        return self.op(eng, "dma_start", wait=wait, inc=inc, incn=16, out=out, in_=in_, **kw)

    def wait(self, eng, *conds):
        conds = tuple((SEM_ALIAS.get(s, s), v) for s, v in conds)
        for s, _ in conds:
            self.sem(s)

        def run(h, conds=conds):
            for s, v in conds:
                if v > 0:
                    h.wait_ge(self.semh[s], v)
        run.meta = (eng, "wait", conds, ())
        self.q[eng].append(run)

    def val(self, sem):
        return self.semc.get(SEM_ALIAS.get(sem, sem), 0)

    def check(self):
        pos = {e: 0 for e in ENGINES}
        cnt = {s: 0 for s in self.semc}
        progress = True
        while progress:
            progress = False
            for e in ENGINES:
                while pos[e] < len(self.q[e]):
                    f = self.q[e][pos[e]]
                    meta = getattr(f, "meta", None)
                    if meta is not None:
                        _, meth, wait, incs = meta
                        if any(cnt[s] < v for s, v in wait):
                            break
                        for s, n in incs:
                            cnt[s] += n
                    pos[e] += 1
                    progress = True
        stuck = {e: pos[e] for e in ENGINES if pos[e] < len(self.q[e])}
        if stuck:
            msg = []
            for e, i in stuck.items():
                meta = getattr(self.q[e][i], "meta", None)
                if meta:
                    _, meth, wait, incs = meta
                    unmet = [(s, v, cnt[s]) for s, v in wait if cnt[s] < v]
                    msg.append(f"{e}[{i}] {meth} unmet={unmet}")
                else:
                    msg.append(f"{e}[{i}] (wait-only)")
            raise RuntimeError("DEADLOCK:\n" + "\n".join(msg))
        return True

    def build(self):
        with self.nc.Block() as blk:
            for e in ENGINES:
                if self.q[e]:
                    getattr(blk, e)(lambda h, e=e: [f(h) for f in self.q[e]])
        self.es.close()
        return self.nc


def build_p1(stages="full", ntiles=16, p=None):
    p = p or Prog("p1")
    p.phase("p1_")
    EI, EO = "ExternalInput", "ExternalOutput"
    xin = p.dram("xin", (B, C, D), F32, EI)
    WkT = p.dram("WkT", (D, M0), F32, EI)
    WvT = p.dram("WvT", (D, M0), F32, EI)
    WqT = p.dram("WqT", (D, M0), F32, EI)
    WgT = p.dram("WgT", (128, 8), F32, EI)
    rw = p.dram("rw", (D, 8), F32, EI)
    rbr = p.dram("rbr", (1, 8), F32, EI)
    bg = p.dram("bg", (1, 1), F32, EI)
    identf = p.dram("identf", (128, 128), F32, EI)
    onesrow = p.dram("onesrow", (1, 128), F32, EI)

    o_xnfm = p.dram("xnfm", (B, 8, 128, C), BF16, EO)
    o_vfm = p.dram("vfm", (B, 2, 128, C), BF16, EO)
    o_qfm = p.dram("qfm", (B, 2, 128, C), BF16, EO)
    o_ktm = p.dram("ktm", (B, 8, 128, M0), BF16, EO)
    o_vtm = p.dram("vtm", (B, 8, 128, M0), BF16, EO)
    o_chat = p.dram("chat", (B, C), F32, EO)
    o_ccol = p.dram("ccol", (B, 128, 8), F32, EO)
    o_L = p.dram("Lrow", (B, 1), F32, EO)
    o_S = p.dram("S", (B, 2, 128, M0), F32, EO)
    o_wrows = p.dram("wrows", (B, 8, C), F32, EO)

    xt = [p.sbuf(f"xt{i}", (128, D), F32) for i in range(2)]
    xnt = [p.sbuf(f"xnt{i}", (128, D), F32) for i in range(2)]
    xf32 = [p.sbuf(f"xf32_{i}", (128, 128), F32) for i in range(2)]
    xnfm = p.sbuf("xnfm_s", (128, B * 8 * C), BF16)

    def XN(b, dc):
        return xnfm[:, (b * 8 + dc) * C:(b * 8 + dc + 1) * C]
    wkT = p.sbuf("wkT", (128, 8 * M0), BF16)
    wvT = p.sbuf("wvT", (128, 8 * M0), BF16)
    wqT = p.sbuf("wqT", (128, 8 * M0), BF16)
    wgc = p.sbuf("wgc", (128, 8), F32)
    rwc = p.sbuf("rwc", (128, 8 * 8), F32)
    rbs = p.sbuf("rbs", (1, 8), F32)
    bgs = p.sbuf("bgs", (1, 1), F32)
    idf = p.sbuf("idf", (128, 128), F32)
    ons = p.sbuf("ons", (1, 128), F32)
    stat = p.sbuf("stat", (128, 8), F32)
    zrow = p.sbuf("zrow", (64, C), F32)
    sgrow = p.sbuf("sgrow", (64, C), F32)
    lgrow = p.sbuf("lgrow", (64, C), F32)
    zc64 = p.sbuf("zc64", (64, C), F32)
    chat = p.sbuf("chat_s", (64, C), F32)
    Lcol = p.sbuf("Lcol", (64, 1), F32)
    bgcol = p.sbuf("bgcol", (64, 1), F32)
    ccols = p.sbuf("ccols", (128, 16), F32)
    rsc = p.sbuf("rsc", (128, 40), F32)
    rstat = p.sbuf("rstat", (128, 8), F32)
    wrs = p.sbuf("wrs", (8, B * C), F32)
    vfm = p.sbuf("vfm_s", (128, B * 2 * C), BF16)
    qfm = p.sbuf("qfm_s", (128, B * 2 * C), BF16)
    ktm = p.sbuf("ktm_s", (128, B * 8 * M0), BF16)
    vtm = p.sbuf("vtm_s", (128, B * 8 * M0), BF16)
    vsc = p.sbuf("vsc", (128, M0), BF16)
    esc = p.sbuf("esc", (128, 4), F32)
    Ssb = p.sbuf("Ssb", (128, 2 * M0), F32)
    wtile = p.sbuf("wtile", (128, 8), F32)
    Lst = p.sbuf("Lst", (1, 2), F32)

    def KT(b, tt):
        return ktm[:, (b * 8 + tt) * M0:(b * 8 + tt + 1) * M0]

    def VT(b, tt):
        return vtm[:, (b * 8 + tt) * M0:(b * 8 + tt + 1) * M0]

    ptr = [p.psum(f"ptr{i}") for i in range(2)]
    prt = p.psum("prt")
    pgt = p.psum("pgt")
    pS = [p.psum(f"pS{i}") for i in range(2)]
    pw = p.psum("pw")
    pmisc = p.psum("pmisc")

    p.dma("gpsimd", wkT.rearrange("p (c m) -> p c m", c=8), WkT.rearrange("(c p) m -> p c m", p=128), inc="wldg")
    p.dma("gpsimd", wvT.rearrange("p (c m) -> p c m", c=8), WvT.rearrange("(c p) m -> p c m", p=128), inc="wldg")
    p.dma("gpsimd", wqT.rearrange("p (c m) -> p c m", c=8), WqT.rearrange("(c p) m -> p c m", p=128), inc="wldg")
    p.dma("sync", wgc[:], WgT, inc="wld")
    p.dma("sync", rwc.rearrange("p (c e) -> p c e", c=8), rw.rearrange("(c p) e -> p c e", p=128), inc="wld")
    p.dma("sync", rbs[:], rbr, inc="wld")
    p.dma("sync", bgs[:], bg, inc="wld")
    p.dma("sync", idf[:], identf, inc="wld")
    p.dma("sync", ons[:], onesrow, inc="wld")
    WRDY = p.val("wld")
    WRDYG = p.val("wldg")
    p.op("vector", "memset", zc64[:], 0.0)
    p.op("vector", "memset", zrow[:], 0.0, inc="zin")
    p.op("tensor", "matmul", pmisc[0:64, 0:1], ons[0:1, 0:64], bgs[:], start=True, stop=True,
         wait=[("wld", WRDY), ("wldg", WRDYG)], inc="tp")
    p.op("vector", "tensor_copy", bgcol[:], pmisc[0:64, 0:1], wait=[("tp", p.val("tp"))], inc="bgr")

    tplast = {}
    for b in range(B):
        for tt in range(8):
            i = b * 8 + tt
            if i >= ntiles:
                continue
            pp = i % 2
            guards = [("xfree", i - 1), ("tp", tplast.get(i - 1, 0))] if i >= 2 else []
            ld = p.dma("sync", xt[pp][:], xin[b, tt * 128:(tt + 1) * 128, :], wait=guards, inc=f"ld{pp}")
            v0 = p.op("vector", "reduce_sum", stat[:, 0:1], xt[pp][:], AX, wait=[(f"ld{pp}", ld)], inc="vp")
            sq = p.op("scalar", "activation", xnt[pp][:], xt[pp][:], AF.Square,
                      accum_out=stat[:, 1:2], wait=[(f"ld{pp}", ld), ("tp", tplast.get(i - 1, 0))], inc="sp")
            v1_ = p.op("vector", "tensor_scalar", stat[:, 2:3], stat[:, 0:1], 1.0 / D, None, ALU.mult,
                       wait=[("sp", sq), ("vp", v0)], inc="vp")
            v2_ = p.op("vector", "tensor_scalar", stat[:, 3:4], stat[:, 1:2], 1.0 / D, None, ALU.mult, inc="vp")
            v3_ = p.op("vector", "tensor_tensor", stat[:, 4:5], stat[:, 2:3], stat[:, 2:3], ALU.mult,
                       wait=[("vp", v1_)], inc="vp")
            v4_ = p.op("vector", "tensor_tensor", stat[:, 3:4], stat[:, 3:4], stat[:, 4:5], ALU.subtract,
                       wait=[("vp", v3_)], inc="vp")
            vs = p.op("vector", "tensor_scalar", stat[:, 3:4], stat[:, 3:4], 1.0, 1e-5, ALU.mult, ALU.add,
                      wait=[("vp", v4_)], inc="vp")
            ss = p.op("scalar", "activation", stat[:, 4:5], stat[:, 3:4], AF.Sqrt, wait=[("vp", vs)], inc="sp")
            vr = p.op("vector", "reciprocal", stat[:, 5:6], stat[:, 4:5], wait=[("sp", ss)], inc="vp")
            v5_ = p.op("vector", "tensor_tensor", stat[:, 6:7], stat[:, 2:3], stat[:, 5:6], ALU.mult,
                       wait=[("vp", vr)], inc="vp")
            vb = p.op("vector", "tensor_scalar", stat[:, 6:7], stat[:, 6:7], -1.0, None, ALU.mult,
                      wait=[("vp", v5_)], inc="vp")
            sx = p.op("vector", "tensor_scalar", xnt[pp][:], xt[pp][:], stat[:, 5:6], stat[:, 6:7],
                      ALU.mult, ALU.add, wait=[("vp", vb)], inc="vp")
            p.op("vector", "nop", inc="xfree")
            SUB = os.environ.get("P1SUB", "all")
            if SUB == "ln":
                continue
            evv0 = p.val("evv")
            evs0 = p.val("evs")
            for dc in range(8 if SUB != "tr0" else 1):
                bk = dc % 2
                w = [("vp", sx), ("evv", evv0 + dc - 1), ("evs", evs0 + dc - 1), ("wld", WRDY), ("wldg", WRDYG)]
                tm = p.op("tensor", "transpose", ptr[bk][:, 0:128], xnt[pp][:, dc * 128:(dc + 1) * 128],
                          idf[:], wait=w, inc="tp")
                if dc == 7:
                    tplast[i] = tm
                e1 = p.op("vector", "tensor_copy", xf32[bk][:], ptr[bk][:, 0:128], wait=[("tp", tm)], inc="evv")
                p.op("scalar", "activation", XN(b, dc)[:, tt * 128:(tt + 1) * 128], xf32[bk][:],
                     AF.Copy, wait=[("evv", e1)], inc="evs")
                if SUB in ("tr", "tr0"):
                    continue
                rw_wait = [("evv", e1)] + ([("rfree", i)] if dc == 0 else [])
                p.op("tensor", "matmul", prt[:, 0:8], xf32[bk][:], rwc[:, dc * 8:(dc + 1) * 8],
                     start=(dc == 0), stop=False, wait=rw_wait)
                gw = [("gfree", i)] if dc == 0 else []
                lastg = p.op("tensor", "matmul", pgt[0:1, 0:128], wgc[:, dc:dc + 1], xf32[bk][:],
                             start=(dc == 0), stop=(dc == 7), wait=gw, inc="tp")
            if SUB in ("tr", "tr0"):
                continue
            tr = p.op("tensor", "matmul", prt[:, 0:8], ons[:], rbs[:], start=False, stop=True, inc="tp")
            p.op("vector", "tensor_copy", zrow[b * 32:b * 32 + 1, tt * 128:(tt + 1) * 128],
                 pgt[0:1, 0:128], wait=[("tp", tr), ("zin", 1)], inc="gfree")
            if SUB == "mm":
                p.op("vector", "tensor_copy", rsc[:, 0:8], prt[:, 0:8], wait=[("tp", tr)], inc="rfree")
                continue
            cr = p.op("vector", "tensor_copy", rsc[:, 0:8], prt[:, 0:8], wait=[("tp", tr)], inc="rfree")
            v1 = p.op("vector", "reduce_max", rstat[:, 0:1], rsc[:, 0:8], AX, wait=[("rfree", cr)], inc="vp")
            v2 = p.op("vector", "tensor_scalar", rstat[:, 1:2], rstat[:, 0:1], -1.0, None, ALU.mult,
                      wait=[("vp", v1)], inc="vp")
            s1 = p.op("scalar", "activation", rsc[:, 8:16], rsc[:, 0:8], AF.Exp,
                      bias=rstat[:, 1:2], wait=[("vp", v2)], inc="sp")
            v3 = p.op("vector", "reduce_sum", rstat[:, 2:3], rsc[:, 8:16], AX, wait=[("sp", s1)], inc="vp")
            v4 = p.op("vector", "reciprocal", rstat[:, 3:4], rstat[:, 2:3], wait=[("vp", v3)], inc="vp")
            v5 = p.op("vector", "tensor_scalar", rsc[:, 32:40], rsc[:, 8:16], rstat[:, 3:4], None, ALU.mult,
                      wait=[("vp", v4)], inc="vp")
            v6 = p.op("vector", "reduce_max", rstat[:, 4:5], rsc[:, 32:40], AX, wait=[("vp", v5)], inc="vp")
            v7 = p.op("vector", "tensor_scalar", rsc[:, 16:24], rsc[:, 32:40], rstat[:, 4:5], None, ALU.is_ge,
                      wait=[("vp", v6)], inc="vp")
            v8 = p.op("vector", "scalar_tensor_tensor", rsc[:, 24:32], rsc[:, 16:24], NEG, rsc[:, 32:40],
                      ALU.mult, ALU.add, wait=[("vp", v7)], inc="vp")
            v9 = p.op("vector", "reduce_max", rstat[:, 5:6], rsc[:, 24:32], AX, wait=[("vp", v8)], inc="vp")
            va = p.op("vector", "tensor_scalar", rsc[:, 16:24], rsc[:, 32:40], rstat[:, 5:6], None, ALU.is_ge,
                      wait=[("vp", v9)], inc="vp")
            vw = p.op("vector", "tensor_tensor", wtile[:, 0:8], rsc[:, 16:24], rsc[:, 32:40], ALU.mult,
                      wait=[("vp", va)], inc="vp")
            twv = p.op("tensor", "transpose", pw[0:8, 0:128], wtile[:, 0:8], idf[:], wait=[("vp", vw), ("wrdone", i)], inc="tp")
            p.op("vector", "tensor_copy", wrs[0:8, b * C + tt * 128:b * C + (tt + 1) * 128],
                 pw[0:8, 0:128], wait=[("tp", twv)], inc="wrdone")

    if stages in ("G", "B", "C", "full"):
        sg1 = p.op("scalar", "activation", sgrow[:], zrow[:], AF.Sigmoid, bias=bgcol[0:64, 0:1],
                   wait=[("vp", p.val("vp")), ("tp", p.val("tp")), ("evv", p.val("evv")), ("evs", p.val("evs")), ("bgr", p.val("bgr"))], inc="sp")
        vg1 = p.op("vector", "tensor_scalar", sgrow[:], sgrow[:], 1e-8, None, ALU.add, wait=[("sp", sg1)], inc="vp")
        sg2 = p.op("scalar", "activation", lgrow[:], sgrow[:], AF.Ln, wait=[("vp", vg1)], inc="sp")
        vsc1 = p.op("vector", "tensor_tensor_scan", chat[:], lgrow[:], zc64[:], 0.0, ALU.add, ALU.add,
                    wait=[("sp", sg2)], inc="vp")
        vL = p.op("vector", "tensor_copy", Lcol[:], chat[:, C - 1:C], wait=[("vp", vsc1)], inc="vp")
        p.op("vector", "tensor_copy", Lst[0:1, 0:1], chat[0:1, C - 1:C])
        vL = p.op("vector", "tensor_copy", Lst[0:1, 1:2], chat[32:33, C - 1:C], inc="vp")

    for b in (range(B) if stages in ("B", "C", "full") else ()):
        xnrdy = [("evv", p.val("evv")), ("evs", p.val("evs"))]
        for (wT, dst, scale_out) in ((wvT, vfm, 1.0), (wqT, qfm, 1.0)):
            for mc in range(2):
                for jt in range(2):
                    bk = jt % 2
                    for dc in range(8):
                        tpj = p.op("tensor", "matmul", pS[bk][:, 0:512],
                                   wT[:, dc * M0 + mc * 128: dc * M0 + (mc + 1) * 128],
                                   XN(b, dc)[:, jt * 512:(jt + 1) * 512],
                                   start=(dc == 0), stop=(dc == 7),
                                   wait=(xnrdy + [("pj", p.val("pj") - 1)]) if dc == 0 else [],
                                   inc="tp" if dc == 7 else None)
                    dsl = dst[:, (b * 2 + mc) * C + jt * 512:(b * 2 + mc) * C + (jt + 1) * 512]
                    p.op("scalar", "activation", dsl, pS[bk][:, 0:512], AF.Copy,
                         wait=[("tp", tpj)], inc="pj")
        for tt in range(8):
            for wi, (wT, dstf) in enumerate(((wkT, KT), (wvT, VT))):
                bk = (2 * tt + wi) % 2
                for dc in range(8):
                    tpj = p.op("tensor", "matmul", ptr[bk][:, 0:M0],
                               XN(b, dc)[:, tt * 128:(tt + 1) * 128],
                               wT[:, dc * M0:(dc + 1) * M0],
                               start=(dc == 0), stop=(dc == 7),
                               wait=[("pj", p.val("pj") - 1)] if dc == 0 else [],
                               inc="tp" if dc == 7 else None)
                p.op("scalar", "activation", dstf(b, tt)[:], ptr[bk][:, 0:M0], AF.Copy,
                     wait=[("tp", tpj)], inc="pj")

    for b in (range(B) if stages in ("C", "full") else ()):
        row = b * 32
        for tt in range(8):
            tc = p.op("tensor", "transpose", pmisc[0:128, 0:1], chat[row:row + 1, tt * 128:(tt + 1) * 128],
                      idf[row:row + 1, row:row + 1],
                      wait=[("vp", vL), ("cfree", p.val("cfree")), ("sfree", p.val("sfree"))], inc="tp")
            p.op("vector", "tensor_copy", ccols[:, b * 8 + tt:b * 8 + tt + 1], pmisc[0:128, 0:1],
                 wait=[("tp", tc)], inc="cfree")
        for tt in range(8):
            tb = p.op("tensor", "matmul", pmisc[0:128, 4:5], ons[:], Lst[0:1, b:b + 1],
                      start=True, stop=True, wait=[("cfree", p.val("cfree")), ("sfree", p.val("sfree"))], inc="tp")
            ve = p.op("vector", "tensor_tensor", esc[:, 0:1], pmisc[0:128, 4:5],
                      ccols[:, b * 8 + tt:b * 8 + tt + 1], ALU.subtract,
                      wait=[("tp", tb), ("sp", p.val("sp"))], inc="sfree")
            se = p.op("scalar", "activation", esc[:, 1:2], esc[:, 0:1], AF.Exp,
                      wait=[("sfree", ve), ("vp", p.val("vp"))], inc="sp")
            vv = p.op("vector", "tensor_scalar", vsc[:], VT(b, tt)[:], esc[:, 1:2], None, ALU.mult,
                      wait=[("sp", se), ("pj", p.val("pj")), ("smm", p.val("smm")),
                            ("sdone", p.val("sdone"))], inc="vp")
            for mvc in range(2):
                r_ = p.op("tensor", "matmul", pS[mvc][:, 0:M0], vsc[:, mvc * 128:(mvc + 1) * 128], KT(b, tt)[:],
                          start=(tt == 0), stop=(tt == 7), wait=[("vp", vv)] if mvc == 0 else [],
                          inc="smm" if mvc == 1 else None)
                if tt == 7 and mvc == 1:
                    tS = r_
        for mvc in range(2):
            p.op("vector", "tensor_copy", Ssb[:, mvc * M0:(mvc + 1) * M0], pS[mvc][:, 0:M0],
                 wait=[("smm", tS), ("st", p.val("st"))], inc="sdone")
        p.sem("sdone")
        p.dma("sync", o_S[b, 0], Ssb[:, 0:M0], wait=[("sdone", p.val("sdone"))], inc="st")
        p.dma("sync", o_S[b, 1], Ssb[:, M0:2 * M0], inc="st")

    allc = [("pj", p.val("pj")), ("vp", p.val("vp")), ("sp", p.val("sp")),
            ("evv", p.val("evv")), ("evs", p.val("evs")), ("wrdone", p.val("wrdone")),
            ("cfree", p.val("cfree"))]
    p.wait("sync", *allc)
    p.dma("sync", o_xnfm.rearrange("b c p t -> p (b c) t"), xnfm.rearrange("p (x t) -> p x t", x=B * 8), inc="st")
    if stages in ("B", "C", "full"):
        p.dma("sync", o_vfm.rearrange("b c p t -> p (b c) t"), vfm.rearrange("p (x t) -> p x t", x=B * 2), inc="st")
        p.dma("sync", o_qfm.rearrange("b c p t -> p (b c) t"), qfm.rearrange("p (x t) -> p x t", x=B * 2), inc="st")
        p.dma("sync", o_ktm.rearrange("b c p t -> p (b c) t"), ktm.rearrange("p (x t) -> p x t", x=B * 8), inc="st")
        p.dma("sync", o_vtm.rearrange("b c p t -> p (b c) t"), vtm.rearrange("p (x t) -> p x t", x=B * 8), inc="st")
    for b in (range(B) if stages in ("G", "B", "C", "full") else ()):
        p.dma("sync", o_chat[b:b + 1, :], chat[b * 32:b * 32 + 1, :], inc="st")
        p.dma("sync", o_L[b:b + 1, :], Lcol[b * 32:b * 32 + 1, :], inc="st")
    for b in (range(B) if stages in ("full", "C") else ()):
        p.dma("sync", o_ccol[b], ccols[:, b * 8:(b + 1) * 8], inc="st")
    for b in range(B):
        p.dma("sync", o_wrows[b], wrs[:, b * C:(b + 1) * C], inc="st")
    p.wait("sync", ("st", p.val("st")))
    return p


def _consts():
    c = {}
    c["identf"] = np.eye(128, dtype=np.float32)
    c["onesrow"] = np.ones((1, 128), dtype=np.float32)
    return c


def p1_inmaps(inputs):
    cst = _consts()
    common = {
        "WkT": np.ascontiguousarray(inputs["L0_attn_Wk"].T),
        "WvT": np.ascontiguousarray(inputs["L0_attn_Wv"].T),
        "WqT": np.ascontiguousarray(inputs["L0_attn_Wq"].T),
        "WgT": np.ascontiguousarray(inputs["L0_attn_Wg"].reshape(8, 128).T),
        "rw": np.ascontiguousarray(inputs["L0_ffn_router_w"]),
        "rbr": np.ascontiguousarray(inputs["L0_ffn_router_b"].reshape(1, 8)),
        "bg": np.ascontiguousarray(inputs["L0_attn_bg"].reshape(1, 1)),
        "identf": cst["identf"],
        "onesrow": cst["onesrow"],
    }
    maps = []
    x = inputs["L0_attn_x"]
    for k in range(NCORE):
        m = dict(common)
        m["xin"] = np.ascontiguousarray(x[:, k * C:(k + 1) * C, :])
        maps.append(m)
    return maps


def kernel(**inputs):
    raise NotImplementedError("under construction")


RSQT = 1.0 / math.sqrt(float(T))
RS16 = 1.0 / 16.0


def build_p2a(p=None):
    p = p or Prog("p2a")
    p.phase("p2a_")
    EI, EO = "ExternalInput", "ExternalOutput"
    i_vfm = p.dram("vfm", (B, 2, 128, C), BF16, EI)
    i_qfm = p.dram("qfm", (B, 2, 128, C), BF16, EI)
    i_ktm = p.dram("ktm", (B, 8, 128, M0), BF16, EI)
    i_chat = p.dram("chat", (B, C), F32, EI)
    i_ccol = p.dram("ccol", (B, 128, 8), F32, EI)
    i_Sall = p.dram("Sall", (NCORE, B, 2, 128, M0), F32, EI)
    i_Lall = p.dram("Lall", (NCORE, B), F32, EI)
    i_mlt = p.dram("masklt", (1, NCORE), F32, EI)
    i_koh = p.dram("konehot", (1, NCORE), F32, EI)
    i_WoT = p.dram("WoT", (M0, D), F32, EI)
    i_mask = p.dram("l0mask", (4, 128, 512), F32, EI)
    identf = p.dram("identf", (128, 128), F32, EI)
    onesrow = p.dram("onesrow", (1, 128), F32, EI)
    o_sfm = p.dram("sfm", (B, 8, 128, C), BF16, EO)

    vfm = p.sbuf("vfm_s", (128, B * 2 * C), BF16)
    qfm = p.sbuf("qfm_s", (128, B * 2 * C), BF16)
    ktm = p.sbuf("ktm_s", (128, B * 8 * M0), BF16)
    chat = p.sbuf("chat_s", (64, C), F32)
    ccols = p.sbuf("ccols", (128, 16), F32)
    nccols = p.sbuf("nccols", (128, 16), F32)
    Sm = [p.sbuf(f"Sm{i}", (128, B * 2 * M0), F32) for i in range(2)]
    Pk = [p.sbuf(f"Pk{i}", (128, B * 2 * M0), F32) for i in range(2)]
    Pbf = p.sbuf("Pbf", (128, B * 2 * M0), BF16)
    Lr = p.sbuf("Lr", (33, 8), F32)
    csum = p.sbuf("csum", (33, 8), F32)
    zc = p.sbuf("zc", (33, 8), F32)
    mlt = p.sbuf("mlt", (33, 8), F32)
    koh = p.sbuf("koh", (33, 8), F32)
    ck1 = p.sbuf("ck1", (33, 1), F32)
    warg = p.sbuf("warg", (33, 8), F32)
    w8 = p.sbuf("w8", (33, 8), F32)
    w8r0 = p.sbuf("w8r0", (1, 16), F32)
    wcol = p.sbuf("wcol", (128, 1), F32)
    ecrow = p.sbuf("ecrow", (1, B * C), F32)
    crow0 = p.sbuf("crow0", (1, B * C), F32)
    tmpr = p.sbuf("tmpr", (1, C), F32)
    woT = p.sbuf("woT", (128, 2 * D), BF16)
    mask = p.sbuf("mask_s", (128, 4 * 512), F32)
    idf = p.sbuf("idf", (128, 128), F32)
    ons = p.sbuf("ons", (1, 128), F32)
    qc = p.sbuf("qc", (128, 2 * 512), BF16)
    dec = [p.sbuf(f"dec{i}", (128, 512), F32) for i in range(2)]
    darg = [p.sbuf(f"darg{i}", (128, 512), F32) for i in range(2)]
    wd = [p.sbuf(f"wd{i}", (128, 512), BF16) for i in range(2)]
    rd = p.sbuf("rd", (128, 2 * 512), BF16)
    sst = p.sbuf("sst", (128, 8 * 512), BF16)

    pR = [p.psum(f"pR{i}") for i in range(2)]
    pSc = [p.psum(f"pSc{i}") for i in range(2)]
    pB = [p.psum(f"pB{i}") for i in range(2)]
    pO = [p.psum(f"pO{i}") for i in range(2)]

    mz = p.op("vector", "memset", Lr[:], 0.0, inc="mz")
    mz = p.op("vector", "memset", mlt[:], 0.0, inc="mz")
    mz = p.op("vector", "memset", koh[:], 0.0, inc="mz")
    p.dma("sync", vfm.rearrange("p (x t) -> p x t", x=B * 2), i_vfm.rearrange("b c p t -> p (b c) t"), inc="ld")
    p.dma("sync", qfm.rearrange("p (x t) -> p x t", x=B * 2), i_qfm.rearrange("b c p t -> p (b c) t"), inc="ld")
    p.dma("sync", ktm.rearrange("p (x t) -> p x t", x=B * 8), i_ktm.rearrange("b c p t -> p (b c) t"), inc="ld")
    for b in range(B):
        p.dma("sync", chat[b * 32:b * 32 + 1, :], i_chat[b:b + 1, :], inc="ld")
        p.dma("sync", ccols[:, b * 8:(b + 1) * 8], i_ccol[b], inc="ld")
        p.dma("sync", Lr[b * 32:b * 32 + 1, :].rearrange("p t -> p t () ") if False else Lr[b * 32:b * 32 + 1, :],
                 i_Lall.rearrange("m b -> b m")[b:b + 1, :], wait=[("mz", mz)], inc="ld",
                 allow_slow_non_contiguous=True)
        p.dma("sync", mlt[b * 32:b * 32 + 1, :], i_mlt, inc="ld")
        p.dma("sync", koh[b * 32:b * 32 + 1, :], i_koh, inc="ld")
    p.dma("gpsimd", woT.rearrange("p (c d) -> p c d", c=2), i_WoT.rearrange("(c p) d -> p c d", p=128), inc="wldg")
    p.dma("sync", mask.rearrange("p (q f) -> p q f", q=4), i_mask.rearrange("q p f -> p q f"), inc="ld")
    p.dma("sync", idf[:], identf, inc="ld")
    p.dma("sync", ons[:], onesrow, inc="ld")
    LD = p.val("ld")
    LDG = p.val("wldg")

    v = p.op("vector", "memset", zc[:], 0.0, wait=[("ld", LD)], inc="vp")
    p.op("vector", "tensor_scalar", nccols[:], ccols[:], -1.0, None, ALU.mult, inc="vp")
    vsc = p.op("vector", "tensor_tensor_scan", csum[:], Lr[:], zc[:], 0.0, ALU.add, ALU.add, inc="vp")
    v = p.op("vector", "tensor_tensor", warg[:], csum[:], koh[:], ALU.mult, wait=[("vp", vsc)], inc="vp")
    vck = p.op("vector", "reduce_sum", ck1[:], warg[:], AX, inc="vp")
    v = p.op("vector", "tensor_scalar", warg[:], csum[:], ck1[:, 0:1], None, ALU.subtract,
             wait=[("vp", vck)], inc="vp")
    v = p.op("vector", "tensor_scalar", warg[:], warg[:], -1.0, 0.0, ALU.mult, ALU.min, inc="vp")
    sE = p.op("scalar", "activation", w8[:], warg[:], AF.Exp, wait=[("vp", v)], inc="sp")
    vw8 = p.op("vector", "tensor_tensor", w8[:], w8[:], mlt[:], ALU.mult, wait=[("sp", sE)], inc="vp")
    v = p.op("vector", "tensor_copy", w8r0[0:1, 0:8], w8[0:1, 0:8], wait=[("vp", vw8)], inc="vp")
    vw8b = p.op("vector", "tensor_copy", w8r0[0:1, 8:16], w8[32:33, 0:8], inc="vp")
    sE2 = p.op("scalar", "activation", ecrow[0:1, 0:C], chat[0:1, :], AF.Exp, wait=[("ld", LD)], inc="sp")
    vt = p.op("vector", "tensor_copy", tmpr[:], chat[32:33, :], wait=[("ld", LD)], inc="vp")
    sE3 = p.op("scalar", "activation", ecrow[0:1, C:2 * C], tmpr[:], AF.Exp, wait=[("vp", vt)], inc="sp")
    p.op("vector", "tensor_copy", crow0[0:1, 0:C], chat[0:1, :], wait=[("ld", LD)], inc="vp")
    vcr = p.op("vector", "tensor_copy", crow0[0:1, C:2 * C], chat[32:33, :], inc="vp")

    for b in range(B):
        p.op("vector", "memset", Pk[0][:, b * 512:(b + 1) * 512], 0.0, inc="vp")
    PK_INIT = p.val("vp")
    stt_done = {}
    for m in range(NCORE):
        pp = m % 2
        g = [("smc", p.val("smc"))] if m >= 2 else []
        lS = p.dma("sync", Sm[pp].rearrange("p (b c x) -> p b c x", b=B, c=2),
                   i_Sall[m].rearrange("b c p x -> p b c x"), wait=g, inc=f"lS{pp}")
        for b in range(B):
            tw = p.op("tensor", "matmul", pB[0][0:128, 0:1], ons[:], w8r0[0:1, b * 8 + m:b * 8 + m + 1],
                      start=True, stop=True,
                      wait=[("vp", vw8b), ("wcf", p.val("wcf"))], inc="tp")
            vc = p.op("vector", "tensor_copy", wcol[:], pB[0][0:128, 0:1], wait=[("tp", tw), ("smc", p.val("smc"))], inc="wcf")
            src_ = Pk[m % 2][:, b * 512:(b + 1) * 512]
            dst_ = Pk[(m + 1) % 2][:, b * 512:(b + 1) * 512]
            v = p.op("vector", "scalar_tensor_tensor", dst_, Sm[pp][:, b * 512:(b + 1) * 512],
                     wcol[:, 0:1], src_, ALU.mult, ALU.add,
                     wait=[("wcf", vc), (f"lS{pp}", lS), ("vp", PK_INIT)], inc="smc")
    PFIN = p.val("smc")
    vP = p.op("vector", "tensor_copy", Pbf[:], Pk[0][:], wait=[("smc", PFIN)], inc="vp")

    sfm_done = 0
    for b in range(B):
        for jt in range(2):
            tq = p.op("tensor", "matmul", pB[0][:, 0:512], ons[:],
                      ecrow[0:1, b * C + jt * 512:b * C + (jt + 1) * 512],
                      start=True, stop=True,
                      wait=[("sp", max(sE2, sE3)), ("wcf", p.val("wcf")), ("qcf", p.val("qcf")),
                            ("vp", p.val("vp"))], inc="tp")
            vq = []
            for c in range(2):
                vq.append(p.op("vector", "scalar_tensor_tensor", qc[:, c * 512:(c + 1) * 512],
                               qfm[:, (b * 2 + c) * C + jt * 512:(b * 2 + c) * C + (jt + 1) * 512],
                               RS16, pB[0][:, 0:512], ALU.mult, ALU.mult,
                               wait=[("tp", tq)], inc="qcf"))
            for mkc in range(2):
                for c in range(2):
                    p.op("tensor", "matmul", pR[mkc][:, 0:512],
                         Pbf[:, b * 512 + c * 256 + mkc * 128:b * 512 + c * 256 + (mkc + 1) * 128],
                         qc[:, c * 512:(c + 1) * 512],
                         start=(c == 0), stop=False,
                         wait=[("vp", vP), ("qcf", vq[1]), ("rdf", p.val("rdf"))] if c == 0 else [])
            nib = 4 * jt + 4
            tp_reads = {}
            vp_wd = {}
            vp_darg = {}
            sp_dec = {}
            for ib in range(nib):
                ibp = ib % 2
                wsc = [("vp", vp_wd.get(ib - 2, 0))]
                for c in range(2):
                    tsc = p.op("tensor", "matmul", pSc[ibp][:, 0:512],
                               vfm[:, (b * 2 + c) * C + ib * 128:(b * 2 + c) * C + (ib + 1) * 128],
                               qfm[:, (b * 2 + c) * C + jt * 512:(b * 2 + c) * C + (jt + 1) * 512],
                               start=(c == 0), stop=(c == 1), wait=wsc if c == 0 else [],
                               inc="tp" if c == 1 else None)
                cjw = [("vp", vp_darg.get(ib - 2, 0))]
                if ib < 2:
                    cjw.append(("qcf", vq[1]))
                cjw.append(("vp", vcr))
                tcj = p.op("tensor", "matmul", pB[ibp][:, 0:512], ons[:],
                           crow0[0:1, b * C + jt * 512:b * C + (jt + 1) * 512],
                           start=True, stop=True, wait=cjw, inc="tp")
                vd = p.op("vector", "tensor_scalar", darg[ibp][:], pB[ibp][:, 0:512],
                          nccols[:, b * 8 + ib:b * 8 + ib + 1], 0.0, ALU.add, ALU.min,
                          wait=[("tp", tcj), ("sp", sp_dec.get(ib - 2, 0))], inc="vp")
                if ib >= 4 * jt:
                    q = ib - 4 * jt
                    vd = p.op("vector", "tensor_tensor", darg[ibp][:], darg[ibp][:],
                              mask[:, q * 512:(q + 1) * 512], ALU.add, wait=[("vp", vd)], inc="vp")
                vp_darg[ib] = vd
                sd = p.op("scalar", "activation", dec[ibp][:], darg[ibp][:], AF.Exp,
                          wait=[("vp", vd), ("vp", vp_wd.get(ib - 2, 0))], inc="sp")
                sp_dec[ib] = sd
                vw = p.op("vector", "scalar_tensor_tensor", wd[ibp][:], pSc[ibp][:, 0:512],
                          RS16, dec[ibp][:], ALU.mult, ALU.mult,
                          wait=[("sp", sd), ("tp", tsc)], inc="vp")
                vp_wd[ib] = vw
                for mkc in range(2):
                    r_ = p.op("tensor", "matmul", pR[mkc][:, 0:512],
                              ktm[:, (b * 8 + ib) * M0 + mkc * 128:(b * 8 + ib) * M0 + (mkc + 1) * 128],
                              wd[ibp][:],
                              start=False, stop=(ib == nib - 1),
                              wait=[("vp", vw)] if mkc == 0 else [],
                              inc="tp" if (mkc == 1 and ib == nib - 1) else None)
                    if mkc == 1 and ib == nib - 1:
                        tp_reads[jt] = r_
            vrd = []
            for mkc in range(2):
                vrd.append(p.op("vector", "tensor_scalar", rd[:, mkc * 512:(mkc + 1) * 512],
                                pR[mkc][:, 0:512], RSQT, None, ALU.mult,
                                wait=[("tp", tp_reads[jt])], inc="rdf"))
            for dmc in range(8):
                ob = dmc % 2
                for mkc in range(2):
                    r_ = p.op("tensor", "matmul", pO[ob][:, 0:512],
                              woT[:, mkc * D + dmc * 128:mkc * D + (dmc + 1) * 128],
                              rd[:, mkc * 512:(mkc + 1) * 512],
                              start=(mkc == 0), stop=(mkc == 1),
                              wait=([("rdf", vrd[1]), ("sof", max(0, p.val("sof") - 1)),
                                     ("wldg", LDG)]
                                    if mkc == 0 else []),
                              inc="tp" if mkc == 1 else None)
                    if mkc == 1:
                        tso = r_
                p.op("scalar", "activation", sst[:, dmc * 512:(dmc + 1) * 512], pO[ob][:, 0:512],
                     AF.Copy, wait=[("tp", tso)], inc="sof")
            p.dma("sync", o_sfm.rearrange("b c p t -> p b c t")[:, b, :, jt * 512:(jt + 1) * 512],
                  sst.rearrange("p (c t) -> p c t", c=8),
                  wait=[("sof", p.val("sof"))], inc="st")
            sfm_done = p.val("st")
            p.wait("scalar", ("st", sfm_done))
    p.wait("sync", ("st", p.val("st")))
    return p


def p2a_inmaps(inputs, p1outs):
    cst = _consts()
    mk = np.zeros((4, 128, 512), dtype=np.float32)
    for q in range(4):
        mk[q, :, :q * 128] = NEG
        blk = np.where(np.triu(np.ones((128, 128), dtype=np.float32)) > 0, 0.0, NEG)
        mk[q, :, q * 128:(q + 1) * 128] = blk
    Sall = np.stack([o["S"] for o in p1outs])
    Lall = np.stack([o["Lrow"] for o in p1outs])[:, :, 0]
    Lall = np.ascontiguousarray(Lall)
    common = {
        "Sall": Sall, "Lall": Lall,
        "WoT": np.ascontiguousarray(inputs["L0_attn_Wo"].T),
        "l0mask": mk, "identf": cst["identf"], "onesrow": cst["onesrow"],
    }
    maps = []
    for k in range(NCORE):
        m = dict(common)
        m["vfm"] = p1outs[k]["vfm"]
        m["qfm"] = p1outs[k]["qfm"]
        m["ktm"] = p1outs[k]["ktm"]
        m["chat"] = p1outs[k]["chat"]
        m["ccol"] = p1outs[k]["ccol"]
        mlt = np.zeros((1, NCORE), np.float32)
        mlt[0, :k] = 1.0
        koh = np.zeros((1, NCORE), np.float32)
        if k > 0:
            koh[0, k - 1] = 1.0
        m["masklt"] = mlt
        m["konehot"] = koh
        maps.append(m)
    return maps


def build_p2b(p=None):
    p = p or Prog("p2b")
    p.phase("p2b_")
    EI, EO = "ExternalInput", "ExternalOutput"
    i_xnfm = p.dram("xnfm", (B, 8, 128, C), BF16, EI)
    i_wrows = p.dram("wrows", (B, 8, C), F32, EI)
    i_sw1 = p.dram("sw1", (D, 2048), F32, EI)
    i_sw2 = p.dram("sw2", (2048, D), F32, EI)
    i_rw1 = p.dram("rw1", (8, D, 2048), F32, EI)
    i_rw2 = p.dram("rw2", (8, 2048, D), F32, EI)
    i_sb1 = p.dram("sb1T", (128, 16), F32, EI)
    i_sb2 = p.dram("sb2T", (128, 8), F32, EI)
    i_rb1 = p.dram("rb1T", (8, 128, 16), F32, EI)
    i_rb2 = p.dram("rb2T", (8, 128, 8), F32, EI)
    onesrow = p.dram("onesrow", (1, 128), F32, EI)
    o_gs = p.dram("gs", (B, 8, 128, C), BF16, EO)

    xnfm = p.sbuf("xnfm_s", (128, B * 8 * C), BF16)
    ring = [p.sbuf(f"ring{i}", (128, 8 * 1024), BF16) for i in range(6)]
    g = p.sbuf("g_s", (128, 8 * C), F32)
    h1sb = p.sbuf("h1sb", (128, 16 * 512), BF16)
    wb = p.sbuf("wb", (128, C), F32)
    wrow = [p.sbuf(f"wrow{i}", (1, C), F32) for i in range(2)]
    sb1 = p.sbuf("sb1_s", (128, 16), F32)
    sb2 = p.sbuf("sb2_s", (128, 8), F32)
    rb1 = p.sbuf("rb1_s", (128, 8 * 16), F32)
    rb2 = p.sbuf("rb2_s", (128, 8 * 8), F32)
    ons = p.sbuf("ons", (1, 128), F32)
    gst = [p.sbuf(f"gst{i}", (128, C), BF16) for i in range(2)]
    t1s = [p.sbuf(f"t1s{i}", (128, 512), F32) for i in range(2)]

    pH = [p.psum(f"pH{i}") for i in range(2)]
    pE = [p.psum(f"pE{i}") for i in range(2)]
    pWB = p.psum("pWB")

    p.dma("sync", xnfm.rearrange("p (x t) -> p x t", x=B * 8), i_xnfm.rearrange("b c p t -> p (b c) t"), inc="ld")
    p.dma("sync", sb1[:], i_sb1, inc="ld")
    p.dma("sync", sb2[:], i_sb2, inc="ld")
    p.dma("sync", rb1.rearrange("p (e x) -> p e x", e=8), i_rb1.rearrange("e p x -> p e x"), inc="ld")
    p.dma("sync", rb2.rearrange("p (e x) -> p e x", e=8), i_rb2.rearrange("e p x -> p e x"), inc="ld")
    p.dma("sync", ons[:], onesrow, inc="ld")
    LD = p.val("ld")

    slot_last = [0] * 6
    slot_sem = [f"wg{i}" for i in range(6)]
    ci = 0

    def load_chunk(src_ap):
        nonlocal ci
        sl = ci % 6
        tgt = p.dma("gpsimd", ring[sl].rearrange("p (c x) -> p c x", c=8), src_ap,
                    wait=[("tp", slot_last[sl])], inc=slot_sem[sl])
        ci += 1
        return sl, tgt

    for b in range(B):
        for ei in range(9):
            e = ei - 1
            shared = (ei == 0)
            if shared:
                c0 = load_chunk(i_sw1[:, 0:1024].rearrange("(c p) h -> p c h", p=128))
                c1 = load_chunk(i_sw1[:, 1024:2048].rearrange("(c p) h -> p c h", p=128))
                c2 = load_chunk(i_sw2[0:1024, :].rearrange("(c p) h -> p c h", p=128))
                c3 = load_chunk(i_sw2[1024:2048, :].rearrange("(c p) h -> p c h", p=128))
                b1col = sb1
                b2col = sb2
            else:
                c0 = load_chunk(i_rw1[e][:, 0:1024].rearrange("(c p) h -> p c h", p=128))
                c1 = load_chunk(i_rw1[e][:, 1024:2048].rearrange("(c p) h -> p c h", p=128))
                c2 = load_chunk(i_rw2[e][0:1024, :].rearrange("(c p) h -> p c h", p=128))
                c3 = load_chunk(i_rw2[e][1024:2048, :].rearrange("(c p) h -> p c h", p=128))
                b1col = rb1[:, e * 16:(e + 1) * 16]
                b2col = rb2[:, e * 8:(e + 1) * 8]
                pp = (b * 8 + e) % 2
                lw = p.dma("sync", wrow[pp][:], i_wrows[b, e:e + 1, :],
                           wait=[("wbf", max(0, p.val("wbf") - 1))], inc=f"lw{pp}")
                for half in range(2):
                    twb = p.op("tensor", "matmul", pWB[:, 0:512], ons[:],
                               wrow[pp][0:1, half * 512:(half + 1) * 512],
                               start=True, stop=True,
                               wait=[(f"lw{pp}", lw), ("ld", LD), ("wbf", p.val("wbf"))], inc="tp")
                    p.op("vector", "tensor_copy", wb[:, half * 512:(half + 1) * 512], pWB[:, 0:512],
                         wait=[("tp", twb)], inc="wbf")
            WBF = p.val("wbf")
            for jt in range(2):
                gel = {}
                for hc in range(16):
                    half = hc // 8
                    hcl = hc % 8
                    wsl, wv = (c0 if half == 0 else c1)
                    hb = hc % 2
                    for dc in range(8):
                        wl = []
                        if dc == 0:
                            wl = [(slot_sem[wsl], wv), ("ld", LD), ("gel", gel.get(hc - 2, 0))]
                            if hc == 0:
                                wl.append((slot_sem[c1[0]], c1[1]))
                        r_ = p.op("tensor", "matmul", pH[hb][:, 0:512],
                                  ring[wsl][:, dc * 1024 + hcl * 128:dc * 1024 + (hcl + 1) * 128],
                                  xnfm[:, (b * 8 + dc) * C + jt * 512:(b * 8 + dc) * C + (jt + 1) * 512],
                                  start=(dc == 0), stop=(dc == 7),
                                  wait=wl,
                                  inc="tp" if dc == 7 else None)
                    slot_last[wsl] = r_
                    gel[hc] = p.op("scalar", "activation", h1sb[:, hc * 512:(hc + 1) * 512],
                                   pH[hb][:, 0:512], AF.Gelu, bias=b1col[:, hcl + half * 8:hcl + half * 8 + 1],
                                   wait=[("tp", r_)], inc="gel")
                GEL = p.val("gel")
                for dmc in range(8):
                    eb = dmc % 2
                    for hc in range(16):
                        half = hc // 8
                        hcl = hc % 8
                        wsl, wv = (c2 if half == 0 else c3)
                        wl = []
                        if hc == 0:
                            wl = [(slot_sem[c2[0]], c2[1]), (slot_sem[c3[0]], c3[1]), ("gel", GEL),
                                  ("gac", max(0, p.val("gac") - (1 if shared else 2)))]
                        r_ = p.op("tensor", "matmul", pE[eb][:, 0:512],
                                  ring[wsl][:, hcl * 1024 + dmc * 128:hcl * 1024 + (dmc + 1) * 128],
                                  h1sb[:, hc * 512:(hc + 1) * 512],
                                  start=(hc == 0), stop=(hc == 15),
                                  wait=wl,
                                  inc="tp" if hc == 15 else None)
                    slot_last[c2[0]] = r_
                    slot_last[c3[0]] = r_
                    gsl = g[:, dmc * C + jt * 512:dmc * C + (jt + 1) * 512]
                    if shared:
                        p.op("vector", "tensor_scalar", gsl, pE[eb][:, 0:512],
                             b2col[:, dmc:dmc + 1], None, ALU.add,
                             wait=[("tp", r_), ("ld", LD)], inc="gac")
                    else:
                        tb_ = dmc % 2
                        tv = p.op("vector", "scalar_tensor_tensor", t1s[tb_][:], pE[eb][:, 0:512],
                                  b2col[:, dmc:dmc + 1], wb[:, jt * 512:(jt + 1) * 512],
                                  ALU.add, ALU.mult,
                                  wait=[("tp", r_), ("wbf", WBF)], inc="gac")
                        p.op("vector", "tensor_tensor", gsl, gsl, t1s[tb_][:], ALU.add, inc="gac")
        GAC = p.val("gac")
        for dmc in range(8):
            gp = dmc % 2
            sv = p.op("scalar", "activation", gst[gp][:], g[:, dmc * C:(dmc + 1) * C], AF.Silu,
                      wait=[("gac", GAC), ("st", max(0, p.val("st") - 16))], inc="sil")
            p.dma("sync", o_gs[b, dmc], gst[gp][:], wait=[("sil", sv)], inc="st")
    p.wait("sync", ("st", p.val("st")))
    return p


def p2b_inmaps(inputs, p1outs):
    cst = _consts()
    common = {
        "sw1": np.ascontiguousarray(inputs["L0_ffn_sw1"]),
        "sw2": np.ascontiguousarray(inputs["L0_ffn_sw2"]),
        "rw1": np.ascontiguousarray(inputs["L0_ffn_rw1"]),
        "rw2": np.ascontiguousarray(inputs["L0_ffn_rw2"]),
        "sb1T": np.ascontiguousarray(inputs["L0_ffn_sb1"].reshape(16, 128).T),
        "sb2T": np.ascontiguousarray(inputs["L0_ffn_sb2"].reshape(8, 128).T),
        "rb1T": np.ascontiguousarray(inputs["L0_ffn_rb1"].reshape(8, 16, 128).transpose(0, 2, 1)),
        "rb2T": np.ascontiguousarray(inputs["L0_ffn_rb2"].reshape(8, 8, 128).transpose(0, 2, 1)),
        "onesrow": cst["onesrow"],
    }
    maps = []
    for k in range(NCORE):
        m = dict(common)
        m["xnfm"] = p1outs[k]["xnfm"]
        m["wrows"] = p1outs[k]["wrows"]
        maps.append(m)
    return maps


H1 = 16
DH1 = 64


def build_p2c(p=None):
    p = p or Prog("p2c")
    p.phase("p2c_")
    EI, EO = "ExternalInput", "ExternalOutput"
    xin = p.dram("xin", (B, C, D), F32, EI)
    i_sfm = p.dram("sfm", (B, 8, 128, C), BF16, EI)
    i_gs = p.dram("gs", (B, 8, 128, C), BF16, EI)
    i_wq = p.dram("Wq1", (D, D), F32, EI)
    i_wk = p.dram("Wk1", (D, D), F32, EI)
    i_wv = p.dram("Wv1", (D, D), F32, EI)
    i_pkv = p.dram("pkv2", (128, H1 * DH1), F32, EI)
    i_gA = p.dram("gA", (128, 8 * H1), F32, EI)
    i_gB = p.dram("gB", (128, 8 * H1), F32, EI)
    identf = p.dram("identf", (128, 128), F32, EI)
    identb = p.dram("identb", (128, 128), BF16, EI)
    o_x1 = p.dram("x1tm", (B, C, D), F32, EO)
    o_xn1 = p.dram("xn1fm", (B, 8, 128, C), BF16, EO)
    o_q1 = p.dram("q1fm", (B, 8, 128, C), BF16, EO)
    o_k1 = p.dram("k1fm", (B, 8, 128, C), BF16, EO)
    o_v1 = p.dram("v1tm", (B, 8, 128, D), BF16, EO)
    o_pp = p.dram("pptm", (B, H1, 8, 128, DH1), BF16, EO)
    o_A = p.dram("Amat", (B, H1, DH1, DH1), F32, EO)
    o_Bv = p.dram("Bvec", (B, H1, DH1), F32, EO)

    sfm = [p.sbuf(f"sfm_s{i}", (128, 8 * 128), BF16) for i in range(2)]
    gs = [p.sbuf(f"gs_s{i}", (128, 8 * 128), BF16) for i in range(2)]
    xn1 = p.sbuf("xn1_s", (128, B * 8 * C), BF16)
    wq = p.sbuf("wq_s", (128, 8 * D), BF16)
    wk = p.sbuf("wk_s", (128, 8 * D), BF16)
    wv = p.sbuf("wv_s", (128, 8 * D), BF16)
    pkv = p.sbuf("pkv_s", (128, H1 * DH1), BF16)
    gAc = p.sbuf("gA_s", (128, 8 * H1), F32)
    gBc = p.sbuf("gB_s", (128, 8 * H1), BF16)
    idf = p.sbuf("idf", (128, 128), F32)
    idb = p.sbuf("idb", (128, 128), BF16)
    xt = [p.sbuf(f"xt{i}", (128, D), F32) for i in range(2)]
    zt = [p.sbuf(f"zt{i}", (128, 128), BF16) for i in range(2)]
    ztm = [p.sbuf(f"ztm{i}", (128, D), BF16) for i in range(2)]
    x1t = [p.sbuf(f"x1t{i}", (128, D), F32) for i in range(2)]
    xn1t = [p.sbuf(f"xn1t{i}", (128, D), F32) for i in range(2)]
    stat = p.sbuf("stat", (128, 8), F32)
    qst = [p.sbuf(f"qst{i}", (128, 512), BF16) for i in range(2)]
    ppst = [p.sbuf(f"ppst{i}", (128, DH1), BF16) for i in range(2)]
    k1tmb = p.sbuf("k1tmb", (128, 8 * D), BF16)
    v1tmb = p.sbuf("v1tmb", (128, 8 * D), BF16)
    ksc = [p.sbuf(f"ksc{i}", (128, DH1), BF16) for i in range(2)]
    Ast = p.sbuf("Ast", (128, DH1), F32)
    Bst = p.sbuf("Bst", (1, DH1 * 2), F32)

    pT = [p.psum(f"pT{i}", (128, 1024), BF16) for i in range(2)]
    pP = [p.psum(f"pP{i}") for i in range(2)]
    pBv = [p.psum(f"pBv{i}") for i in range(2)]
    pA = [p.psum(f"pA{i}") for i in range(2)]


    p.dma("gpsimd", wq.rearrange("p (c o) -> p c o", c=8), i_wq.rearrange("(c p) o -> p c o", p=128), inc="wldg")
    p.dma("gpsimd", wk.rearrange("p (c o) -> p c o", c=8), i_wk.rearrange("(c p) o -> p c o", p=128), inc="wldg")
    p.dma("gpsimd", wv.rearrange("p (c o) -> p c o", c=8), i_wv.rearrange("(c p) o -> p c o", p=128), inc="wldg")
    p.dma("gpsimd", pkv[:], i_pkv, inc="wldg")
    p.dma("sync", gAc[:], i_gA, inc="ld")
    p.dma("gpsimd", gBc[:], i_gB, inc="wldg")
    p.dma("sync", idf[:], identf, inc="ld")
    p.dma("sync", idb[:], identb, inc="ld")
    LD = p.val("ld")
    LDG = p.val("wldg")

    x1dma = {}
    for b in range(B):
        for tt in range(8):
            i = b * 8 + tt
            pp_ = i % 2
            lx = p.dma("sync", xt[pp_][:], xin[b, tt * 128:(tt + 1) * 128, :],
                       wait=[("x1f", max(0, p.val("x1f") - 1))], inc=f"lx{pp_}")
            zg_ = [("zf", max(0, p.val("zf") - 8))] if i >= 1 else []
            lsf = p.dma("sync", sfm[pp_].rearrange("p (c t) -> p c t", c=8),
                        i_sfm.rearrange("b c p t -> p b c t")[:, b, :, tt * 128:(tt + 1) * 128],
                        wait=zg_, inc=f"lsf{pp_}")
            lgs = p.dma("sync", gs[pp_].rearrange("p (c t) -> p c t", c=8),
                        i_gs.rearrange("b c p t -> p b c t")[:, b, :, tt * 128:(tt + 1) * 128],
                        wait=zg_, inc=f"lgs{pp_}")
            for dc in range(8):
                zb = dc % 2
                vz = p.op("vector", "tensor_tensor", zt[zb][:],
                          sfm[pp_][:, dc * 128:(dc + 1) * 128],
                          gs[pp_][:, dc * 128:(dc + 1) * 128],
                          ALU.mult, wait=[(f"lsf{pp_}", lsf), (f"lgs{pp_}", lgs),
                                          ("ztf", max(0, p.val("ztf") - 1))], inc="zf")
                tz = p.op("tensor", "transpose", pT[zb][:, 0:128], zt[zb][:], idb[:],
                          wait=[("zf", vz), ("ztf", max(0, p.val("ztf") - 1)), ("ld", LD)], inc="tp")
                p.op("vector", "tensor_copy", ztm[pp_][:, dc * 128:(dc + 1) * 128], pT[zb][:, 0:128],
                     wait=[("tp", tz)], inc="ztf")
            ZD = p.val("ztf")
            vx1 = p.op("vector", "tensor_tensor", x1t[pp_][:], xt[pp_][:], ztm[pp_][:], ALU.add,
                       wait=[(f"lx{pp_}", lx), ("ztf", ZD), ("st", x1dma.get(i - 2, 0))], inc="x1f")
            x1dma[i] = p.dma("sync", o_x1[b, tt * 128:(tt + 1) * 128, :], x1t[pp_][:],
                             wait=[("x1f", vx1)], inc="st")
            v0 = p.op("vector", "reduce_sum", stat[:, 0:1], x1t[pp_][:], AX, wait=[("x1f", vx1)], inc="vp")
            sq = p.op("scalar", "activation", xn1t[pp_][:], x1t[pp_][:], AF.Square,
                      accum_out=stat[:, 1:2], wait=[("x1f", vx1), ("tp", p.val("tp"))], inc="sp")
            v1_ = p.op("vector", "tensor_scalar", stat[:, 2:3], stat[:, 0:1], 1.0 / D, None, ALU.mult,
                       wait=[("sp", sq), ("vp", v0)], inc="vp")
            v2_ = p.op("vector", "tensor_scalar", stat[:, 3:4], stat[:, 1:2], 1.0 / D, None, ALU.mult, inc="vp")
            v3_ = p.op("vector", "tensor_tensor", stat[:, 4:5], stat[:, 2:3], stat[:, 2:3], ALU.mult,
                       wait=[("vp", v1_)], inc="vp")
            v4_ = p.op("vector", "tensor_tensor", stat[:, 3:4], stat[:, 3:4], stat[:, 4:5], ALU.subtract,
                       wait=[("vp", v3_)], inc="vp")
            vs = p.op("vector", "tensor_scalar", stat[:, 3:4], stat[:, 3:4], 1.0, 1e-5, ALU.mult, ALU.add,
                      wait=[("vp", v4_)], inc="vp")
            ss = p.op("scalar", "activation", stat[:, 4:5], stat[:, 3:4], AF.Sqrt, wait=[("vp", vs)], inc="sp")
            vr = p.op("vector", "reciprocal", stat[:, 5:6], stat[:, 4:5], wait=[("sp", ss)], inc="vp")
            v5_ = p.op("vector", "tensor_tensor", stat[:, 6:7], stat[:, 2:3], stat[:, 5:6], ALU.mult,
                       wait=[("vp", vr)], inc="vp")
            vb = p.op("vector", "tensor_scalar", stat[:, 6:7], stat[:, 6:7], -1.0, None, ALU.mult,
                      wait=[("vp", v5_)], inc="vp")
            sx = p.op("vector", "tensor_scalar", xn1t[pp_][:], x1t[pp_][:], stat[:, 5:6], stat[:, 6:7],
                      ALU.mult, ALU.add, wait=[("vp", vb), ("sp", sq)], inc="vp")
            for dc in range(8):
                zb = dc % 2
                tn = p.op("tensor", "transpose", pT[zb][128:128, 0:128] if False else pT[zb][:, 128:256],
                          None, None) if False else None
            for dc in range(8):
                zb = dc % 2
                tn = p.op("tensor", "transpose", pP[zb][:, 0:128], xn1t[pp_][:, dc * 128:(dc + 1) * 128],
                          idf[:], wait=[("vp", sx), ("xnf", max(0, p.val("xnf") - 1)), ("ld", LD)], inc="tp")
                p.op("scalar", "activation", xn1[:, (b * 8 + dc) * C + tt * 128:(b * 8 + dc) * C + (tt + 1) * 128],
                     pP[zb][:, 0:128], AF.Copy, wait=[("tp", tn)], inc="xnf")
    XN1 = p.val("xnf")
    p.dma("sync", o_xn1.rearrange("b c p t -> p (b c) t"), xn1.rearrange("p (x t) -> p x t", x=B * 8),
          wait=[("xnf", XN1)], inc="st")
    if p.fused:
        o_h0c = p.dram("halo0c", (B * 8 * 128, HALO), BF16, "Internal")
        h0v = o_h0c.rearrange("(b c p) t -> p (b c) t", b=B, c=8)
        p.dma("sync", h0v,
              xn1.rearrange("p (x t) -> p x t", x=B * 8)[:, :, C - HALO:], inc="st")

    qdma = {}
    ppdma = {}
    bvdma = {}
    adma = {}
    for b in range(B):
        for oc in range(8):
            for (widx, (wsb, dst)) in enumerate(((wq, o_q1), (wk, o_k1))):
                for jt in range(2):
                    qb = (oc * 4 + widx * 2 + jt) % 2
                    for dc in range(8):
                        r_ = p.op("tensor", "matmul", pT[qb][:, 0:512].bitcast(F32) if False else pP[qb][:, 0:512],
                                  wsb[:, dc * D + oc * 128:dc * D + (oc + 1) * 128],
                                  xn1[:, (b * 8 + dc) * C + jt * 512:(b * 8 + dc) * C + (jt + 1) * 512],
                                  start=(dc == 0), stop=(dc == 7),
                                  wait=([("xnf", XN1), ("wldg", LDG), ("qev", max(0, p.val("qev") - 1))]
                                        if dc == 0 else []),
                                  inc="tp" if dc == 7 else None)
                    vq = p.op("vector", "tensor_copy", qst[qb][:], pP[qb][:, 0:512],
                              wait=[("tp", r_), ("st", qdma.get(qb, 0))], inc="qev")
                    qdma[qb] = p.dma("sync", dst[b, oc, :, jt * 512:(jt + 1) * 512], qst[qb][:],
                                     wait=[("qev", vq)], inc="st")
                    if widx == 0:
                        for hh in range(2):
                            h = oc * 2 + hh
                            for st_ in range(4):
                                s4 = jt * 4 + st_
                                ppb = s4 % 2
                                tpp = p.op("tensor", "matmul", pA[ppb][:, 0:DH1],
                                           qst[qb][hh * 64:(hh + 1) * 64, st_ * 128:(st_ + 1) * 128],
                                           pkv[hh * 64:(hh + 1) * 64, h * DH1:(h + 1) * DH1],
                                           start=True, stop=True,
                                           wait=[("qev", vq), ("ppe", max(0, p.val("ppe") - 1))],
                                           inc="tp")
                                vpp = p.op("vector", "tensor_copy", ppst[ppb][:], pA[ppb][:, 0:DH1],
                                           wait=[("tp", tpp), ("st", ppdma.get(ppb, 0))], inc="ppe")
                                ppdma[ppb] = p.dma("sync", o_pp[b, h, s4], ppst[ppb][:],
                                                   wait=[("ppe", vpp)], inc="st")
                                p.op("tensor", "matmul", pBv[hh][0:1, 0:DH1],
                                     gBc[:, s4 * H1 + h:s4 * H1 + h + 1],
                                     ppst[ppb][:],
                                     start=(s4 == 0), stop=(s4 == 7),
                                     wait=[("ppe", vpp)] + ([("bve", max(0, p.val("bve") - 1))] if s4 == 0 else []),
                                     inc="tp" if s4 == 7 else None)
            for hh in range(2):
                h = oc * 2 + hh
                vbv = p.op("vector", "tensor_copy", Bst[0:1, hh * DH1:(hh + 1) * DH1], pBv[hh][0:1, 0:DH1],
                           wait=[("tp", p.val("tp")), ("st", bvdma.get(hh, 0))], inc="bve")
                bvdma[hh] = p.dma("sync", o_Bv[b, h:h + 1, :], Bst[0:1, hh * DH1:(hh + 1) * DH1],
                                  wait=[("bve", vbv)], inc="st")
    for b in range(B):
        for tt in range(8):
            for (widx, wsb) in ((0, wk), (1, wv)):
                dstb = k1tmb if widx == 0 else v1tmb
                for ov in range(2):
                    vb_ = (tt * 4 + widx * 2 + ov) % 2
                    for dc in range(8):
                        r_ = p.op("tensor", "matmul", pP[vb_][:, 0:512],
                                  xn1[:, (b * 8 + dc) * C + tt * 128:(b * 8 + dc) * C + (tt + 1) * 128],
                                  wsb[:, dc * D + ov * 512:dc * D + (ov + 1) * 512],
                                  start=(dc == 0), stop=(dc == 7),
                                  wait=([("xnf", XN1), ("tve", max(0, p.val("tve") - 1)),
                                         ("amm", p.val("amm")), ("qev", p.val("qev")),
                                         ("ppe", p.val("ppe"))] if dc == 0 else []),
                                  inc="tp" if dc == 7 else None)
                    vv = p.op("vector", "tensor_copy", dstb[:, tt * D + ov * 512:tt * D + (ov + 1) * 512],
                              pP[vb_][:, 0:512], wait=[("tp", r_)], inc="tve")
                    if widx == 1:
                        p.dma("sync", o_v1[b, tt, :, ov * 512:(ov + 1) * 512],
                              dstb[:, tt * D + ov * 512:tt * D + (ov + 1) * 512],
                              wait=[("tve", vv)], inc="st")
        TVE = p.val("tve")
        for h in range(H1):
            ab = h % 2
            for tt in range(8):
                kb = tt % 2
                vks = p.op("vector", "tensor_scalar", ksc[kb][:],
                           k1tmb[:, tt * D + h * DH1:tt * D + (h + 1) * DH1],
                           gAc[:, tt * H1 + h:tt * H1 + h + 1], None, ALU.mult,
                           wait=[("tve", TVE), ("ld", LD), ("amm", max(0, p.val("amm") - 1))], inc="ksf")
                r_ = p.op("tensor", "matmul", pBv[ab][0:DH1, 0:DH1], ksc[kb][:],
                          v1tmb[:, tt * D + h * DH1:tt * D + (h + 1) * DH1],
                          start=(tt == 0), stop=(tt == 7),
                          wait=([("ksf", vks)] + ([("bve", p.val("bve")), ("aev", max(0, p.val("aev") - 1))]
                                                  if tt == 0 else [])),
                          inc="amm")
            vA = p.op("vector", "tensor_copy", Ast[0:DH1, :], pBv[ab][0:DH1, 0:DH1],
                      wait=[("amm", p.val("amm")), ("st", adma.get(0, 0))], inc="aev")
            adma[0] = p.dma("sync", o_A[b, h], Ast[0:DH1, :], wait=[("aev", vA)], inc="st")
    p.wait("sync", ("st", p.val("st")))
    return p


def p2c_inmaps(inputs, p1outs, p2aouts, p2bouts):
    cst = _consts()
    gamma = (1.0 - 2.0 ** (-5.0 - np.arange(H1))).astype(np.float64)
    s = np.arange(C)
    gA = np.zeros((128, 8, H1), np.float32)
    gB = np.zeros((128, 8, H1), np.float32)
    for t8 in range(8):
        sv = s[t8 * 128:(t8 + 1) * 128]
        for h in range(H1):
            gA[:, t8, h] = gamma[h] ** (C - 1 - sv)
            gB[:, t8, h] = gamma[h] ** sv
    pkv = inputs["L1_attn_past_kv"].astype(np.float32)
    pk64 = pkv.transpose(1, 0, 2).reshape(64, H1 * DH1)
    pk128 = np.concatenate([pk64, pk64], axis=0)
    common = {
        "Wq1": np.ascontiguousarray(inputs["L1_attn_Wq"]),
        "Wk1": np.ascontiguousarray(inputs["L1_attn_Wk"]),
        "Wv1": np.ascontiguousarray(inputs["L1_attn_Wv"]),
        "pkv2": np.ascontiguousarray(pk128),
        "gA": gA.reshape(128, 8 * H1), "gB": gB.reshape(128, 8 * H1),
        "identf": cst["identf"], "identb": cst["identf"].astype(np.float32),
    }
    import ml_dtypes
    common["identb"] = np.eye(128, dtype=ml_dtypes.bfloat16)
    maps = []
    xfull = inputs["L0_attn_x"]
    for k in range(NCORE):
        m = dict(common)
        m["xin"] = np.ascontiguousarray(xfull[:, k * C:(k + 1) * C, :])
        m["sfm"] = p2aouts[k]["sfm"]
        m["gs"] = p2bouts[k]["gs"]
        maps.append(m)
    return maps


def build_p3a(p=None):
    p = p or Prog("p3a")
    p.phase("p3a_")
    EI, EO = "ExternalInput", "ExternalOutput"
    i_q = p.dram("q1fm", (B, 8, 128, C), BF16, EI)
    i_k = p.dram("k1fm", (B, 8, 128, C), BF16, EI)
    i_v = p.dram("v1tm", (B, 8, 128, D), BF16, EI)
    i_pp = p.dram("pptm", (B, H1, 8, 128, DH1), BF16, EI)
    i_Aall = p.dram("Aall", (NCORE, B, H1, DH1, DH1), F32, EI)
    i_Bvall = p.dram("Bvall", (NCORE, B, H1, DH1), F32, EI)
    i_D512 = p.dram("D512", (H1, 11, 128, 512), F32, EI)
    i_RD512 = p.dram("RD512", (H1, 8, 128, 512), F32, EI)
    i_gqt = p.dram("gqt", (H1, 128, C), F32, EI)
    i_gsuf = p.dram("gsuf", (H1, 1, C), F32, EI)
    i_wA = p.dram("wA128", (128, 8 * H1), F32, EI)
    i_wS = p.dram("wS8", (8, H1), F32, EI)
    i_ones = p.dram("onescol", (64, 1), F32, EI)
    o_ret = p.dram("ret", (B, 8, 128, C), BF16, EO)
    o_stat = p.dram("statp", (1, 2 * B), F32, EO)

    qf = p.sbuf("qf_s", (128, B * 8 * C), BF16)
    kf = p.sbuf("kf_s", (128, B * 8 * C), BF16)
    vt = p.sbuf("vt_s", (128, B * 8 * D), BF16)
    ppt = [p.sbuf(f"ppt_s{i}", (128, B * 8 * DH1), BF16) for i in range(2)]
    Aall = p.sbuf("Aall_s", (64, NCORE * B * H1 * DH1), BF16)
    Bvall = p.sbuf("Bvall_s", (8, B * H1 * DH1), F32)
    Dh = [p.sbuf(f"Dh{i}", (128, 11 * 512), BF16) for i in range(2)]
    RDh = [p.sbuf(f"RDh{i}", (128, 8 * 512), BF16) for i in range(2)]
    gqh = [p.sbuf(f"gqh{i}", (128, C), BF16) for i in range(2)]
    qg = [p.sbuf(f"qg{i}", (64, 512), BF16) for i in range(2)]
    gsuf = [p.sbuf(f"gsuf_s{i}", (1, C), BF16) for i in range(2)]
    wA = p.sbuf("wA_s", (128, 8 * H1), F32)
    wS = p.sbuf("wS_s", (8, H1), F32)
    onc = p.sbuf("onc", (64, 1), F32)
    Pc = [p.sbuf(f"Pc{i}", (64, DH1), F32) for i in range(2)]
    Pbf = p.sbuf("Pbf_s", (64, DH1), BF16)
    Qrow = p.sbuf("Qrow", (1, DH1), BF16)
    retd = [p.sbuf(f"retd{i}", (128, 512), BF16) for i in range(2)]
    retst = p.sbuf("retst", (64, C), BF16)
    sqs = p.sbuf("sqs", (64, C), F32)
    srun = p.sbuf("srun", (64, 4), F32)
    stt = p.sbuf("stt", (64, 4), F32)
    stf = p.sbuf("stf", (1, 4), F32)

    pret = [p.psum(f"pret{i}") for i in range(2)]
    pqk = [p.psum(f"pqk{i}") for i in range(2)]
    pQv = p.psum("pQv")
    pst = p.psum("pst")

    p.dma("sync", qf.rearrange("p (x t) -> p x t", x=B * 8), i_q.rearrange("b c p t -> p (b c) t"), inc="ld")
    p.dma("sync", kf.rearrange("p (x t) -> p x t", x=B * 8), i_k.rearrange("b c p t -> p (b c) t"), inc="ld")
    p.dma("sync", vt.rearrange("p (x t) -> p x t", x=B * 8), i_v.rearrange("b c p t -> p (b c) t"), inc="ld")
    for m_ in range(NCORE):
        p.dma("gpsimd", Aall.rearrange("p (m x e) -> p m x e", m=NCORE, x=B * H1)[:, m_],
              i_Aall[m_].rearrange("b h p e -> p (b h) e"), inc="wldg")
    p.dma("sync", Bvall.rearrange("p (x e) -> p x e", x=B * H1), i_Bvall.rearrange("m b h e -> m (b h) e"), inc="ld")
    p.dma("sync", wA[:], i_wA, inc="ld")
    p.dma("sync", wS[:], i_wS, inc="ld")
    p.dma("sync", onc[:], i_ones, inc="ld")
    LD = p.val("ld")
    LDG = p.val("wldg")

    p.op("vector", "memset", srun[:], 0.0, inc="vp")
    SR0 = p.val("vp")

    def PPT(hp, b, sb):
        off = (b * 8 + sb) * DH1
        return ppt[hp][:, off:off + DH1]

    retdma = {}
    hlast = {}
    for h in range(H1):
        hp = h % 2
        hb2 = (h % 2) * 64
        hc = h // 2
        gD = [("dcf", p.val("dcf"))] if h >= 2 else []
        gT = [("tp", hlast.get(h - 2, 0))]
        lD = p.dma("gpsimd", Dh[hp].rearrange("p (v t) -> p v t", v=11), i_D512[h].rearrange("v p t -> p v t"), wait=gD + gT, inc=f"lD{hp}")
        lR = p.dma("gpsimd", RDh[hp].rearrange("p (v t) -> p v t", v=8), i_RD512[h].rearrange("v p t -> p v t"), inc=f"lD{hp}")
        lG = p.dma("gpsimd", gqh[hp][:], i_gqt[h], inc=f"lD{hp}")
        lS2 = p.dma("gpsimd", gsuf[hp][:], i_gsuf[h], inc=f"lD{hp}")
        for b_ in range(B):
            lP = p.dma("sync", ppt[hp].rearrange("p (b s e) -> p b s e", b=B, s=8)[:, b_],
                       i_pp[b_, h].rearrange("s p e -> p s e"), wait=gT if b_ == 0 else [], inc=f"lP{hp}")
        for b in range(B):
            vp0 = p.op("vector", "memset", Pc[0][:], 0.0, wait=[("ld", LD), ("wldg", LDG)], inc="vp")
            last = vp0
            for m in range(NCORE):
                src_ = Pc[m % 2]
                dst_ = Pc[(m + 1) % 2]
                last = p.op("vector", "scalar_tensor_tensor", dst_[:],
                            Aall[:, ((m * B + b) * H1 + h) * DH1:((m * B + b) * H1 + h + 1) * DH1],
                            wA[0:64, m * H1 + h:m * H1 + h + 1], src_[:], ALU.mult, ALU.add,
                            wait=[("vp", last)], inc="vp")
            vPb = p.op("vector", "tensor_copy", Pbf[:], Pc[0][:], wait=[("vp", last)], inc="vp")
            tQ = p.op("tensor", "matmul", pQv[0:1, 0:DH1], wS[:, h:h + 1],
                      Bvall[:, (b * H1 + h) * DH1:(b * H1 + h + 1) * DH1],
                      start=True, stop=True, wait=[("ld", LD), ("qrf", p.val("qrf"))], inc="tp")
            vQ = p.op("vector", "tensor_copy", Qrow[:], pQv[0:1, 0:DH1], wait=[("tp", tQ)], inc="qrf")
            tt_last = {}
            for tt in range(2):
                rb_ = tt % 2
                vqg = p.op("vector", "tensor_tensor", qg[rb_][:],
                           qf[hb2:hb2 + 64, (b * 8 + hc) * C + tt * 512:(b * 8 + hc) * C + (tt + 1) * 512],
                           gqh[hp][hb2:hb2 + 64, tt * 512:(tt + 1) * 512], ALU.mult,
                           wait=[(f"lD{hp}", lG), ("ld", LD), ("qgf", max(0, p.val("qgf") - 1))], inc="qgf")
                t0 = p.op("tensor", "matmul", pret[rb_][0:64, 0:512], Pbf[:], qg[rb_][:],
                          start=True, stop=False,
                          wait=[("vp", vPb), ("qgf", vqg), ("rte", max(0, p.val("rte") - 2))], inc="tp")
                p.op("tensor", "matmul", pret[rb_][0:64, 0:512], Qrow[:],
                     gsuf[hp][0:1, tt * 512:(tt + 1) * 512],
                     start=False, stop=False, wait=[("qrf", vQ), (f"lD{hp}", lS2)])
                nqk = {}
                tqv = {}
                nsb = 4 * tt + 4

                def emit_tq2(sb):
                    qb_ = sb % 2
                    tqv[sb] = p.op("tensor", "matmul", pqk[qb_][:, 0:512],
                                   kf[hb2:hb2 + 64, (b * 8 + hc) * C + sb * 128:(b * 8 + hc) * C + (sb + 1) * 128],
                                   qf[hb2:hb2 + 64, (b * 8 + hc) * C + tt * 512:(b * 8 + hc) * C + (tt + 1) * 512],
                                   start=True, stop=True,
                                   wait=[("dcf", nqk.get(sb - 2, 0)), ("ld", LD)], inc="tp")

                emit_tq2(0)
                for sb in range(nsb):
                    qb_ = sb % 2
                    if sb + 1 < nsb:
                        emit_tq2(sb + 1)
                    d0 = 4 * tt - sb
                    vdd = p.op("vector", "tensor_tensor", retd[qb_][:], pqk[qb_][:, 0:512],
                               Dh[hp][:, (d0 + 3) * 512:(d0 + 4) * 512], ALU.mult,
                               wait=[("tp", tqv[sb]), (f"lD{hp}", lD), ("dcf", max(0, p.val("dcf") - 1))],
                               inc="dcf")
                    nqk[sb] = vdd
                    p.op("tensor", "matmul", pret[rb_][0:64, 0:512],
                         vt[:, (b * 8 + sb) * D + h * DH1:(b * 8 + sb) * D + (h + 1) * DH1],
                         retd[qb_][:],
                         start=False, stop=False, wait=[("dcf", vdd)])
                rev_first = True
                for sb in range(4 * tt, 8):
                    d0p = sb - 4 * tt
                    isl = (sb == 7)
                    r_ = p.op("tensor", "matmul", pret[rb_][0:64, 0:512], PPT(hp, b, sb),
                              RDh[hp][:, d0p * 512:(d0p + 1) * 512],
                              start=False, stop=isl,
                              wait=[(f"lP{hp}", lP), (f"lD{hp}", lR)] if rev_first else [],
                              inc="tp" if isl else None)
                    rev_first = False
                tt_last[tt] = r_
                hlast[h] = r_
            for tt in range(2):
                rb_ = tt % 2
                vre = p.op("vector", "tensor_copy", retst[:, tt * 512:(tt + 1) * 512],
                           pret[rb_][0:64, 0:512], wait=[("tp", tt_last[1]), ("st", retdma.get(0, 0))],
                           inc="rte")
            RTE = p.val("rte")
            retdma[0] = p.dma("sync", o_ret[b, hc, hb2:hb2 + 64, :], retst[0:64, :],
                              wait=[("rte", RTE)], inc="st")
            vs1 = p.op("vector", "reduce_sum", stt[:, 0:1], retst[0:64, :], AX, wait=[("rte", RTE)], inc="vp")
            ss1 = p.op("scalar", "activation", sqs[:], retst[0:64, :], AF.Square,
                       accum_out=stt[:, 1:2], wait=[("rte", RTE), ("sqf", p.val("sqf"))], inc="sqf")
            v_ = p.op("vector", "tensor_tensor", srun[:, b * 2:b * 2 + 1], srun[:, b * 2:b * 2 + 1],
                      stt[:, 0:1], ALU.add, wait=[("vp", vs1), ("vp", SR0), ("sqf", ss1)], inc="vp")
            p.op("vector", "tensor_tensor", srun[:, b * 2 + 1:b * 2 + 2], srun[:, b * 2 + 1:b * 2 + 2],
                 stt[:, 1:2], ALU.add, inc="vp")
    VEND = p.val("vp")
    for j in range(4):
        tst = p.op("tensor", "matmul", pst[j:j + 1, 0:1] if False else pst[0:1, j:j + 1],
                   srun[:, j:j + 1], onc[:], start=True, stop=True,
                   wait=[("vp", VEND), ("ld", LD)], inc="tp")
    TST = p.val("tp")
    vf = p.op("vector", "tensor_copy", stf[0:1, 0:4], pst[0:1, 0:4], wait=[("tp", TST)], inc="vp")
    p.dma("sync", o_stat[0:1, :], stf[0:1, 0:4], wait=[("vp", p.val("vp"))], inc="st")
    p.wait("sync", ("st", p.val("st")))
    return p


def p3a_inmaps(inputs, p2couts):
    gamma = (1.0 - 2.0 ** (-5.0 - np.arange(H1)))
    sarr = np.arange(128, dtype=np.float64)
    D512 = np.zeros((H1, 11, 128, 512), np.float32)
    RD512 = np.zeros((H1, 8, 128, 512), np.float32)
    for h in range(H1):
        lg = np.log(gamma[h])
        for iv, d0 in enumerate(range(-3, 8)):
            for q in range(4):
                dd = 128 * (d0 + q) + sarr[None, :] - sarr[:, None]
                D512[h, iv, :, q * 128:(q + 1) * 128] = np.where(dd >= 0, np.exp(lg * np.maximum(dd, 0)), 0.0)
        for d0p in range(8):
            for q in range(4):
                dd = 128 * (d0p - q) + sarr[:, None] - sarr[None, :]
                RD512[h, d0p, :, q * 128:(q + 1) * 128] = np.where(dd >= 0, np.exp(lg * np.maximum(dd, 0)), 0.0)
    tl = np.arange(C, dtype=np.float64)
    gqt = np.zeros((H1, 128, C), np.float32)
    gsuf = np.zeros((H1, 1, C), np.float32)
    for h in range(H1):
        gqt[h, :, :] = (gamma[h] ** (tl + 1))[None, :]
        gsuf[h, 0, :] = gamma[h] ** (C - tl)
    common = {
        "D512": D512, "RD512": RD512, "gqt": gqt, "gsuf": gsuf,
        "onescol": np.ones((64, 1), np.float32),
    }
    Aall = np.stack([o["Amat"] for o in p2couts])
    Bvall = np.stack([o["Bvec"] for o in p2couts])
    common["Aall"] = Aall
    common["Bvall"] = Bvall
    maps = []
    for k in range(NCORE):
        m = dict(common)
        wA = np.zeros((128, 8, H1), np.float32)
        wS = np.zeros((8, H1), np.float32)
        for mm in range(NCORE):
            for h in range(H1):
                if mm < k:
                    wA[:, mm, h] = gamma[h] ** ((k - 1 - mm) * C)
                if mm > k:
                    wS[mm, h] = gamma[h] ** ((mm - k - 1) * C)
        m["wA128"] = wA.reshape(128, 8 * H1)
        m["wS8"] = wS
        m["q1fm"] = p2couts[k]["q1fm"]
        m["k1fm"] = p2couts[k]["k1fm"]
        m["v1tm"] = p2couts[k]["v1tm"]
        m["pptm"] = p2couts[k]["pptm"]
        maps.append(m)
    return maps



HALO = 642
FB = 13
FTOK = FB * 128
QTOK = 1152
INNER = 512


def build_subA(l, p=None):
    p = p or Prog(f"sub{l}A")
    p.phase(f"sA{l}_")
    EI, EO = "ExternalInput", "ExternalOutput"
    xdt = BF16 if l == 0 else F32
    i_xi = p.dram("xn1fm" if l == 0 else "xint1", (B, 8, 128, C), xdt, EI)
    i_xh = None if p.fused else p.dram("xihalo", (B, 8, 128, HALO), xdt, EI)
    i_wqkv = p.dram("wqkv", (D, 3 * INNER), F32, EI)
    i_wo = p.dram("wo", (INNER, D), F32, EI)
    i_csq = p.dram("csq", (128, FTOK), F32, EI)
    i_ssq = p.dram("ssq", (128, FTOK), F32, EI)
    i_csk = p.dram("csk", (128, FTOK), F32, EI)
    i_ssk = p.dram("ssk", (128, FTOK), F32, EI)
    i_mA = p.dram("maskA", (9, 128, 640), F32, EI)
    i_H2 = p.dram("H2", (128, 128), F32, EI)
    i_H2T = p.dram("H2T", (128, 128), F32, EI)
    identf = p.dram("identf", (128, 128), F32, EI)
    identb = p.dram("identb", (128, 128), BF16, EI)
    onesrow = p.dram("onesrow", (1, 128), F32, EI)
    i_hv = p.dram("hvcol", (128, 1), F32, EI)
    o_xm = p.dram(f"xmid{l}", (B, 8, 128, 1 + C), F32, EO)

    xr = p.sbuf("xr", (128, 8 * B * (HALO + C)), BF16)

    def XR(b, dc):
        off = (dc * B + b) * (HALO + C)
        return xr[:, off:off + HALO + C]

    def XRS(b, dc, fb, n=128):
        shift = 0 if dc < 4 else 1
        st = 2 - shift + fb * 128
        return XR(b, dc)[:, st:st + n]
    wqkv = p.sbuf("wqkv_s", (128, 8 * 1536), BF16)
    wo = p.sbuf("wo_s", (128, 4 * D), BF16)
    csq = p.sbuf("csq_s", (128, FTOK), BF16)
    ssq = p.sbuf("ssq_s", (128, FTOK), BF16)
    csk = p.sbuf("csk_s", (128, FTOK), BF16)
    ssk = p.sbuf("ssk_s", (128, FTOK), BF16)
    mA = p.sbuf("mA_s", (128, 9 * 640), BF16)
    H2 = p.sbuf("H2_s", (128, 128), F32)
    H2T = p.sbuf("H2T_s", (128, 128), F32)
    idb = p.sbuf("idb", (128, 128), BF16)
    hv = p.sbuf("hv_s", (128, 1), F32)
    praw = p.sbuf("praw", (128, 2816), F32)
    qh = p.sbuf("qh", (128, 1536), F32)
    rth = p.sbuf("rth", (128, 1024), F32)
    sqsw = p.sbuf("sqsw", (128, 2816), F32)
    nrw = sqsw[0:2, :]
    qrot = p.sbuf("qrot", (128, QTOK), BF16)
    krot = p.sbuf("krot", (128, FTOK), BF16)
    vhat = p.sbuf("vhat", (128, B * FB * 520 + 64), BF16)
    expq = p.sbuf("expq", (128, 4 * 640), BF16)
    kTt = p.sbuf("kTt", (128, 2 * 640), BF16)
    ofm = p.sbuf("ofm", (128, 4 * B * QTOK), BF16)

    def OF(b, ch):
        off = (ch * B + b) * QTOK
        return ofm[:, off:off + QTOK]
    negm = p.sbuf("negm", (128, 16), F32)
    dnm = p.sbuf("dnm", (128, 8), F32)
    xfs = [p.sbuf(f"xfs{i}", (128, 512), F32) for i in range(2)]
    xst = [p.sbuf(f"xst{i}", (128, 512), F32) for i in range(2)]
    hcl = p.sbuf("hcl", (128, B * 8), F32)

    SIMB = [p.psum(f"sim{i}", (128, 1024)) for i in range(3)]
    TR = p.psum("tr", (128, 512))
    TRb = TR[:].bitcast(BF16)
    PVb = p.psum("pvb", (128, 512))
    p.npsum += 3

    if p.fused:
        hg = p.dram(f"halo{l}g", (NCORE, B, 8, 128, HALO), xdt, "Internal")
        ccv = p.val("cc")
        hsem = f"ldh{l}"
        hq = "sync" if xdt == BF16 else "gpsimd"

        def halo_switch(h, hg=hg, ccv=ccv, hsem=hsem):
            if ccv > 0:
                h.wait_ge(p.semh["cc"], ccv)
            if os.environ.get("NOSWITCH"):
                for b in range(B):
                    for dc in range(8):
                        h.dma_start(out=XR(b, dc)[:, 0:HALO], in_=hg[6, b, dc]).then_inc(
                            p.semh[hsem], 16)
                return
            for arm in p.nc.switch_core_id(engines=[h]):
                rank = (arm.logical - 1) % NCORE
                for b in range(B):
                    for dc in range(8):
                        h.dma_start(out=XR(b, dc)[:, 0:HALO], in_=hg[rank, b, dc]).then_inc(
                            p.semh[hsem], 16)
        halo_switch.meta = (hq, "halo_switch", (("cc", ccv),) if ccv > 0 else (),
                            ((hsem, 16 * B * 8),))
        p.sem(hsem)
        p.q[hq].append(halo_switch)
        p.semc[hsem] = p.val(hsem) + 16 * B * 8
        LH = p.val(hsem)
        for b in range(B):
            for dc in range(8):
                if xdt == BF16:
                    p.dma("sync", XR(b, dc)[:, HALO:HALO + C], i_xi[b, dc], inc="ld")
                else:
                    p.dma("gpsimd", XR(b, dc)[:, HALO:HALO + C], i_xi[b, dc], inc="wldg")
    else:
        for b in range(B):
            for dc in range(8):
                if xdt == BF16:
                    p.dma("sync", XR(b, dc)[:, 0:HALO], i_xh[b, dc], inc="ld")
                    p.dma("sync", XR(b, dc)[:, HALO:HALO + C], i_xi[b, dc], inc="ld")
                else:
                    p.dma("gpsimd", XR(b, dc)[:, 0:HALO], i_xh[b, dc], inc="wldg")
                    p.dma("gpsimd", XR(b, dc)[:, HALO:HALO + C], i_xi[b, dc], inc="wldg")
        LH = 0
    p.dma("gpsimd", wqkv.rearrange("p (c o) -> p c o", c=8), i_wqkv.rearrange("(c p) o -> p c o", p=128), inc="wldg")
    p.dma("gpsimd", wo.rearrange("p (c o) -> p c o", c=4), i_wo.rearrange("(c p) o -> p c o", p=128), inc="wldg")
    p.dma("gpsimd", csq[:], i_csq, inc="wldg")
    p.dma("gpsimd", ssq[:], i_ssq, inc="wldg")
    p.dma("gpsimd", csk[:], i_csk, inc="wldg")
    p.dma("gpsimd", ssk[:], i_ssk, inc="wldg")
    p.dma("gpsimd", mA.rearrange("p (q t) -> p q t", q=9), i_mA.rearrange("q p t -> p q t"), inc="wldg")
    p.dma("sync", H2[:], i_H2, inc="ld")
    p.dma("sync", H2T[:], i_H2T, inc="ld")
    p.dma("sync", idb[:], identb, inc="ld")
    p.dma("sync", hv[:], i_hv, inc="ld")
    LD = p.val("ld")
    LDG = p.val("wldg")
    ALLW = [("ld", LD), ("wldg", LDG)]
    if p.fused:
        ALLW.append((f"ldh{l}", LH))
    for b in range(B):
        for dc in range(8):
            p.op("vector", "tensor_scalar", XR(b, dc)[:, 0:HALO], XR(b, dc)[:, 0:HALO],
                 hv[:, 0:1], None, ALU.mult, wait=ALLW, inc="hz")
    ALLW = ALLW + [("hz", p.val("hz"))]

    p.op("vector", "memset", vhat[:], 0.0, inc="vhe")
    for b in range(B):
        for fb in range(FB):
            vb_ = (b * FB + fb) % 3
            for dc in range(8):
                r_ = p.op("tensor", "matmul", SIMB[vb_][:, 0:512],
                          XRS(b, dc, fb),
                          wqkv[:, dc * 1536 + 1024:dc * 1536 + 1536],
                          start=(dc == 0), stop=(dc == 7),
                          wait=(ALLW + [("vhe", max(0, p.val("vhe") - 2))]) if dc == 0 else [],
                          inc="tp" if dc == 7 else None)
            vv1 = p.op("vector", "tensor_copy",
                       vhat[:, 0:B * FB * 520].rearrange("p (x s) -> p x s", x=B * FB * 8, s=65)[:, (b * FB + fb) * 8:(b * FB + fb) * 8 + 8, 0:64],
                       SIMB[vb_][:, 0:512].rearrange("p (x s) -> p x s", x=8, s=64),
                       wait=[("tp", r_)], inc="vhe")
    VHE = p.val("vhe")

    def VH(b, fb, h):
        off = ((b * FB + fb) * 8 + h) * 65
        return vhat[:, off:off + 128]

    QGRPS = ((0, 4, 512), (0, 8, 512), (0, 12, 128))
    KGRPS = ((1, 0, 512), (1, 4, 512), (1, 8, 512), (1, 12, 128))
    GLIST = QGRPS + KGRPS

    def PRC(widx, fb0):
        return 1664 + (fb0 - 4) * 128 if widx == 0 else fb0 * 128

    def emit_proj(b, ch, SIMG):
        P0e, P0m, P0o = p.val("pte"), p.val("mqe"), p.val("obe")
        P0t, P0s, P0n = p.val("tev"), p.val("sqf"), p.val("nre")
        gv = {}

        def mm(j):
            widx, fb0, ntok = GLIST[j]
            base = widx * 512 + ch * 128
            w0 = ALLW + [("pte", P0e), ("mqe", P0m), ("vhe", VHE),
                         ("pje", gv.get(("evP", j - 4), 0))]
            r_ = 0
            for dc in range(8):
                r_ = p.op("tensor", "matmul", SIMB[j % 3][:, 0:ntok],
                          wqkv[:, dc * 1536 + base:dc * 1536 + base + 128],
                          XRS(b, dc, fb0, ntok),
                          start=(dc == 0), stop=(dc == 7),
                          wait=w0 if dc == 0 else [],
                          inc="tp" if dc == 7 else None)
            gv[("mm", j)] = r_

        def evP(j):
            widx, fb0, ntok = GLIST[j]
            c0 = PRC(widx, fb0)
            gv[("evP", j)] = p.op("vector", "tensor_copy", praw[:, c0:c0 + ntok],
                                  SIMB[j % 3][:, 0:ntok],
                                  wait=[("tp", gv[("mm", j)]), ("sqf", P0s)], inc="pje")

        def sq(j):
            widx, fb0, ntok = GLIST[j]
            c0 = PRC(widx, fb0)
            gv[("sq", j)] = p.op("scalar", "activation", sqsw[:, c0:c0 + ntok],
                                 praw[:, c0:c0 + ntok], AF.Square,
                                 wait=[("pje", gv[("evP", j)]), ("nre", P0n),
                                       ("tp", SIMG)], inc="sqf")

        def h2(j):
            widx, fb0, ntok = GLIST[j]
            c0 = PRC(widx, fb0)
            gv[("h2", j)] = p.op("tensor", "matmul", PVb[0:128, 0:ntok], H2[:],
                                 sqsw[:, c0:c0 + ntok], start=True, stop=True,
                                 wait=[("sqf", gv[("sq", j)]), ("obe", P0o),
                                       ("nre", gv.get(("evN", j - 1), 0))], inc="tp")

        def evN(j):
            widx, fb0, ntok = GLIST[j]
            c0 = PRC(widx, fb0)
            gv[("evN", j)] = p.op("vector", "tensor_copy", nrw[:, c0:c0 + ntok],
                                  PVb[0:2, 0:ntok],
                                  wait=[("tp", gv[("h2", j)])], inc="nre")

        def mxg(j):
            widx, fb0, ntok = GLIST[j]
            c0 = PRC(widx, fb0)
            gv[("mxg", j)] = p.op("vector", "tensor_scalar", nrw[:, c0:c0 + ntok],
                                  nrw[:, c0:c0 + ntok], 1e-24, None, ALU.max,
                                  wait=[("nre", gv[("evN", j)])], inc="nre")

        def sqg(j):
            widx, fb0, ntok = GLIST[j]
            c0 = PRC(widx, fb0)
            gv[("sqg", j)] = p.op("scalar", "activation", nrw[:, c0:c0 + ntok],
                                  nrw[:, c0:c0 + ntok], AF.Sqrt,
                                  wait=[("nre", gv[("mxg", j)])], inc="sqf")

        def rcg(j):
            widx, fb0, ntok = GLIST[j]
            c0 = PRC(widx, fb0)
            gv[("rcg", j)] = p.op("vector", "reciprocal", nrw[:, c0:c0 + ntok],
                                  nrw[:, c0:c0 + ntok],
                                  wait=[("sqf", gv[("sqg", j)])], inc="nre")

        def bc(j):
            widx, fb0, ntok = GLIST[j]
            c0 = PRC(widx, fb0)
            gv[("bc", j)] = p.op("tensor", "matmul", SIMB[j % 3][:, 512:512 + ntok],
                                 H2T[:], sqsw[0:128, c0:c0 + ntok],
                                 start=True, stop=True,
                                 wait=[("nre", gv[("rcg", j)]), ("pte", P0e),
                                       ("tev", P0t),
                                       ("rote", gv.get(("qm", j - 3), 0))], inc="tp")

        def qm(j):
            widx, fb0, ntok = GLIST[j]
            c0 = PRC(widx, fb0)
            gsl = (j % 3) * 512
            src = SIMB[j % 3][:, 512:512 + ntok]
            gv[("qm", j)] = p.op("vector", "tensor_tensor", qh[:, gsl:gsl + ntok],
                                 praw[:, c0:c0 + ntok], src, ALU.mult,
                                 wait=[("tp", gv[("bc", j)]),
                                       ("sqf", gv.get(("cpsS", j - 3), 0))], inc="rote")

        def cps(j):
            widx, fb0, ntok = GLIST[j]
            gsl = (j % 3) * 512
            gslR = (j % 2) * 512
            cd = cs = 0
            for g4, eng in ((0, "vector"), (1, "scalar"), (2, "vector"), (3, "scalar")):
                b0 = g4 * 32
                srow = (g4 ^ 1) * 32
                if eng == "vector":
                    cd = p.op("vector", "tensor_copy", rth[b0:b0 + 32, gslR:gslR + ntok],
                              qh[srow:srow + 32, gsl:gsl + ntok],
                              wait=[("rote", gv[("qm", j)])], inc="rote")
                else:
                    cs = p.op("scalar", "activation", rth[b0:b0 + 32, gslR:gslR + ntok],
                              qh[srow:srow + 32, gsl:gsl + ntok], AF.Copy,
                              wait=[("rote", gv[("qm", j)]),
                                    ("rote", gv.get(("t3", j - 2), 0))], inc="sqf")
            gv[("cps", j)] = cd
            gv[("cpsS", j)] = cs

        def t123(j):
            widx, fb0, ntok = GLIST[j]
            c0 = fb0 * 128
            gsl = (j % 3) * 512
            gslR = (j % 2) * 512
            rot = qrot if widx == 0 else krot
            cs_, ss_ = (csq, ssq) if widx == 0 else (csk, ssk)
            rotc = c0 - 512 if widx == 0 else c0
            p.op("vector", "tensor_tensor", rot[:, rotc:rotc + ntok],
                 qh[:, gsl:gsl + ntok], cs_[:, c0:c0 + ntok], ALU.mult,
                 wait=[("rote", gv[("cps", j)]), ("wldg", LDG), ("tp", SIMG)], inc="rote")
            p.op("vector", "tensor_tensor", rth[:, gslR:gslR + ntok],
                 rth[:, gslR:gslR + ntok], ss_[:, c0:c0 + ntok], ALU.mult,
                 wait=[("sqf", gv[("cpsS", j)])], inc="rote")
            gv[("t3", j)] = p.op("vector", "tensor_tensor", rot[:, rotc:rotc + ntok],
                                 rot[:, rotc:rotc + ntok], rth[:, gslR:gslR + ntok], ALU.add,
                                 inc="rote")

        J = len(GLIST)
        stages = (mm, evP, sq, h2, evN, mxg, sqg, rcg, bc, qm, cps, t123)
        for t in range(J + len(stages) - 1):
            for k in range(len(stages) - 1, -1, -1):
                j = t - k
                if 0 <= j < J:
                    stages[k](j)
        return p.val("rote")

    NQ = 9

    def emit_attn(b, ch, ROT):
        stv = {}
        U = 2 * NQ

        def HQ(u):
            return (ch * 2 + (u // NQ), (u // NQ) * 64, u % NQ)

        def sim(u):
            h, hb2, qi = HQ(u)
            sb = SIMB[u % 3]
            w0 = [("pte", stv.get(("exp", u - 3), 0)), ("mqe", stv.get(("mx", u - 3), 0)),
                  ("rote", ROT), ("vhe", VHE), ("wldg", LDG)]
            p.op("tensor", "matmul", sb[:, 0:512],
                 qrot[hb2:hb2 + 64, qi * 128:(qi + 1) * 128],
                 krot[hb2:hb2 + 64, qi * 128:qi * 128 + 512],
                 start=True, stop=False, wait=w0)
            p.op("tensor", "matmul", sb[:, 512:640],
                 qrot[hb2:hb2 + 64, qi * 128:(qi + 1) * 128],
                 krot[hb2:hb2 + 64, qi * 128 + 512:qi * 128 + 640],
                 start=True, stop=False)
            mw = 512 if qi <= 4 else 128
            p.op("tensor", "matmul", sb[:, 0:mw], idb[:], mA[:, qi * 640:qi * 640 + mw],
                 start=False, stop=True)
            stv[("sim", u)] = p.op("tensor", "matmul", sb[:, 512:640], idb[:],
                                   mA[:, qi * 640 + 512:qi * 640 + 640],
                                   start=False, stop=True, inc="tp")

        def mx(u):
            sl = (u % 3) * 4
            stv[("mx", u)] = p.op("vector", "tensor_reduce", negm[:, sl + 2:sl + 3],
                                  SIMB[u % 3][:, 0:640], AX, ALU.max, negate=True,
                                  wait=[("tp", stv[("sim", u)]),
                                        ("pte", stv.get(("exp", u - 3), 0))], inc="mqe")

        def expf(u):
            esl = (u % 4) * 640
            bias = negm[:, (u % 3) * 4 + 2:(u % 3) * 4 + 3]
            stv[("exp", u)] = p.op("scalar", "activation", expq[:, esl:esl + 640],
                                   SIMB[u % 3][:, 0:640], AF.Exp, bias=bias,
                                   wait=[("mqe", stv[("mx", u)]),
                                         ("tp", stv.get(("tr", u - 4), 0))], inc="pte")

        def den(u):
            esl = (u % 4) * 640
            dsl = (u % 3) * 2
            p.op("vector", "reduce_sum", dnm[:, dsl:dsl + 1], expq[:, esl:esl + 640], AX,
                 wait=[("pte", stv[("exp", u)]),
                       ("sqf", stv.get(("dap", u - 3), 0))], inc="dne")
            stv[("den", u)] = p.op("vector", "reciprocal", dnm[:, dsl + 1:dsl + 2],
                                   dnm[:, dsl:dsl + 1], inc="dne")

        def dapply(u):
            esl = (u % 4) * 640
            dsl = (u % 3) * 2
            stv[("dap", u)] = p.op("scalar", "activation", expq[:, esl:esl + 640],
                                   expq[:, esl:esl + 640], AF.Copy,
                                   scale=dnm[:, dsl + 1:dsl + 2],
                                   wait=[("dne", stv[("den", u)])], inc="sqf")

        def transp(u):
            esl = (u % 4) * 640
            tsl = 0
            w0 = [("sqf", stv[("dap", u)]), ("tev", stv.get(("ev", u - 1), 0))]
            r_ = 0
            for t in range(5):
                r_ = p.op("tensor", "transpose", TRb[:, tsl + t * 128:tsl + (t + 1) * 128],
                          expq[:, esl + t * 128:esl + (t + 1) * 128], idb[:],
                          wait=w0 if t == 0 else [], inc="tp" if t == 4 else None)
            stv[("tr", u)] = r_

        def evac(u):
            tsl = 0
            ksl = (u % 2) * 640
            stv[("ev", u)] = p.op("scalar", "activation", kTt[:, ksl:ksl + 640],
                                  TRb[:, tsl:tsl + 640], AF.Copy,
                                  wait=[("tp", stv[("tr", u)]),
                                        ("tp", stv.get(("pv", u - 2), 0))], inc="tev")

        def pv(u):
            h, hb2, qi = HQ(u)
            ksl = (u % 2) * 640
            qcol = (qi % 4) * 128
            w0 = [("tev", stv[("ev", u)]), ("obe", p.val("obe"))]
            r_ = 0
            for kbi in range(5):
                r_ = p.op("tensor", "matmul", PVb[0:128, qcol:qcol + 128],
                          VH(b, qi + kbi, h), kTt[:, ksl + kbi * 128:ksl + (kbi + 1) * 128],
                          start=(kbi == 0), stop=(kbi == 4),
                          wait=w0 if kbi == 0 else [],
                          inc="tp" if kbi == 4 else None)
            stv[("pv", u)] = r_

        def ofc(u):
            h, hb2, qi = HQ(u)
            q0 = 8 if qi == 8 else (qi // 4) * 4
            W = (qi - q0 + 1) * 128
            p.op("vector", "tensor_copy", OF(b, ch)[hb2:hb2 + 64, q0 * 128:q0 * 128 + W],
                 PVb[0:64, 0:W], wait=[("tp", stv[("pv", u)])], inc="obe")

        for t in range(U + 8):
            if 0 <= t - 8 < U and (t - 8) % NQ in (3, 7, 8):
                ofc(t - 8)
            if 0 <= t - 7 < U:
                pv(t - 7)
            if 0 <= t - 6 < U:
                evac(t - 6)
            if 0 <= t - 5 < U:
                transp(t - 5)
            if 0 <= t - 4 < U:
                dapply(t - 4)
            if 0 <= t - 3 < U:
                den(t - 3)
            if 0 <= t - 2 < U:
                expf(t - 2)
            if 0 <= t - 1 < U:
                mx(t - 1)
            if t < U:
                sim(t)

    for b in range(B):
        for ch in range(4):
            SIMG = p.val("tp")
            ROT = emit_proj(b, ch, SIMG)
            emit_attn(b, ch, ROT)
    OBE = p.val("obe")

    xmdma = {}
    for b in range(B):
        for dc in range(8):
            hci = p.op("vector", "tensor_copy", hcl[:, b * 8 + dc:b * 8 + dc + 1],
                       XR(b, dc)[:, HALO - 1:HALO], wait=ALLW, inc="hce")
        for tt in range(2):
            for dc in range(8):
                ob_ = (tt * 8 + dc) % 2
                for ic in range(4):
                    r_ = p.op("tensor", "matmul", SIMB[ob_][:, 0:512],
                              wo[:, ic * D + dc * 128:ic * D + (dc + 1) * 128],
                              OF(b, ic)[:, 128 + tt * 512:128 + (tt + 1) * 512],
                              start=(ic == 0), stop=(ic == 3),
                              wait=([("obe", OBE), ("pte", p.val("pte")),
                                     ("mqe", p.val("mqe")),
                                     ("xme", max(0, p.val("xme") - 1)),
                                     ("hce", p.val("hce"))] + ALLW)
                              if ic == 0 else [],
                              inc="tp" if ic == 3 else None)
                pp_ = (tt * 8 + dc) % 2
                if l == 0:
                    vx = p.op("vector", "tensor_tensor", xst[pp_][:], SIMB[ob_][:, 0:512],
                              XR(b, dc)[:, HALO + tt * 512:HALO + (tt + 1) * 512], ALU.add,
                              wait=[("tp", r_), ("st", xmdma.get(pp_, 0))], inc="xme")
                else:
                    lf = p.dma("sync", xfs[pp_][:], i_xi[b, dc, :, tt * 512:(tt + 1) * 512],
                               wait=[("xme", max(0, p.val("xme") - 1))], inc=f"lf{pp_}")
                    vx = p.op("vector", "tensor_tensor", xst[pp_][:], SIMB[ob_][:, 0:512],
                              xfs[pp_][:], ALU.add,
                              wait=[("tp", r_), (f"lf{pp_}", lf), ("st", xmdma.get(pp_, 0))], inc="xme")
                xmdma[pp_] = p.dma("sync", o_xm[b, dc, :, 1 + tt * 512:1 + (tt + 1) * 512], xst[pp_][:],
                                   wait=[("xme", vx)], inc="st")
        for dc in range(8):
            ob_ = dc % 2
            for ic in range(4):
                r_ = p.op("tensor", "matmul", SIMB[ob_][:, 0:128],
                          wo[:, ic * D + dc * 128:ic * D + (dc + 1) * 128],
                          OF(b, ic)[:, 0:128],
                          start=(ic == 0), stop=(ic == 3),
                          wait=[("hce", p.val("hce")), ("xme", p.val("xme"))] if ic == 0 else [],
                          inc="tp" if ic == 3 else None)
            vh_ = p.op("vector", "scalar_tensor_tensor", hcl[:, b * 8 + dc:b * 8 + dc + 1],
                       SIMB[ob_][:, 127:128], hv[:, 0:1], hcl[:, b * 8 + dc:b * 8 + dc + 1],
                       ALU.mult, ALU.add, wait=[("tp", r_), ("ld", LD)], inc="hce")
            p.dma("sync", o_xm[b, dc, :, 0:1], hcl[:, b * 8 + dc:b * 8 + dc + 1],
                  wait=[("hce", vh_)], inc="st", allow_slow_non_contiguous=True)
    p.wait("sync", ("st", p.val("st")))
    return p


def build_subF(l, p=None):
    p = p or Prog(f"sub{l}F")
    p.phase(f"sF{l}_")
    EI, EO = "ExternalInput", "ExternalOutput"
    i_xm = p.dram(f"xmid{l}", (B, 8, 128, 1 + C), F32, EI)
    i_w1a = p.dram("w1a", (D, 2816), F32, EI)
    i_w1g = p.dram("w1g", (D, 2816), F32, EI)
    i_w2 = p.dram("w2p", (2816, D), F32, EI)
    i_b1a = p.dram("b1aT", (128, 22), F32, EI)
    i_b1g = p.dram("b1gT", (128, 22), F32, EI)
    i_b2 = p.dram("b2T", (128, 8), F32, EI)
    i_lng = p.dram("lngT", (128, 8), F32, EI)
    i_lnb = p.dram("lnbT", (128, 8), F32, EI)
    i_oc = p.dram("onescol", (128, 1), F32, EI)
    onesrow = p.dram("onesrow", (1, 128), F32, EI)
    o_x = p.dram("xint1" if l == 0 else "xint2", (B, 8, 128, C), F32, EO)

    w1a = p.sbuf("w1a_s", (128, 8 * 2816), BF16)
    w1g = p.sbuf("w1g_s", (128, 8 * 2816), BF16)
    w2 = p.sbuf("w2_s", (128, 22 * D), BF16)
    b1a = p.sbuf("b1a_s", (128, 22), F32)
    b1g = p.sbuf("b1g_s", (128, 22), F32)
    b2 = p.sbuf("b2_s", (128, 8), F32)
    lng = p.sbuf("lng_s", (128, 8), F32)
    lnb = p.sbuf("lnb_s", (128, 8), F32)
    ocl = p.sbuf("ocl", (128, 1), F32)
    ons = p.sbuf("ons", (1, 128), F32)
    xf = [p.sbuf(f"xf{i}", (128, 8 * 257), F32) for i in range(2)]
    sqs = p.sbuf("sqs3", (128, 256), F32)
    lnr = p.sbuf("lnr", (1, 8 * 256), F32)
    hn = p.sbuf("hn", (128, 8 * 256), BF16)
    gg = p.sbuf("gg", (128, 256), BF16)
    mch = p.sbuf("mch", (128, 22 * 256), BF16)
    xst = [p.sbuf(f"xst{i}", (128, 256), F32) for i in range(2)]

    pb = [p.psum(f"pb{i}") for i in range(8)]

    p.dma("gpsimd", w1a.rearrange("p (c o) -> p c o", c=8), i_w1a.rearrange("(c p) o -> p c o", p=128), inc="wldg")
    p.dma("gpsimd", w1g.rearrange("p (c o) -> p c o", c=8), i_w1g.rearrange("(c p) o -> p c o", p=128), inc="wldg")
    p.dma("gpsimd", w2.rearrange("p (c o) -> p c o", c=22), i_w2.rearrange("(c p) o -> p c o", p=128), inc="wldg")
    p.dma("sync", b1a[:], i_b1a, inc="ld")
    p.dma("sync", b1g[:], i_b1g, inc="ld")
    p.dma("sync", b2[:], i_b2, inc="ld")
    p.dma("sync", lng[:], i_lng, inc="ld")
    p.dma("sync", lnb[:], i_lnb, inc="ld")
    p.dma("sync", ocl[:], i_oc, inc="ld")
    p.dma("sync", ons[:], onesrow, inc="ld")
    LD = p.val("ld")
    LDG = p.val("wldg")

    xodma = {}
    for b in range(B):
        for tg in range(4):
            pp_ = (b * 4 + tg) % 2
            lx = p.dma("sync", xf[pp_].rearrange("p (c t) -> p c t", c=8),
                       i_xm.rearrange("b c p t -> p b c t")[:, b, :, tg * 256:tg * 256 + 257],
                       wait=[("hne", max(0, p.val("hne") - 8))], inc=f"lx{pp_}")
            SH = {dc: (1 if dc < 4 else 0) for dc in range(8)}
            for dc in range(8):
                r_ = p.op("tensor", "matmul", pb[6][0:1, 0:256], ocl[:],
                          xf[pp_][:, dc * 257 + SH[dc]:dc * 257 + SH[dc] + 256],
                          start=(dc == 0), stop=(dc == 7),
                          wait=[(f"lx{pp_}", lx), ("ld", LD), ("lne", p.val("lne"))] if dc == 0 else [],
                          inc="tp" if dc == 7 else None)
            vm = p.op("vector", "tensor_scalar", lnr[0:1, 0:256], pb[6][0:1, 0:256], 1.0 / D, None,
                      ALU.mult, wait=[("tp", r_)], inc="lne")
            for dc in range(8):
                sq_ = p.op("scalar", "activation", sqs[:, 0:256],
                           xf[pp_][:, dc * 257 + SH[dc]:dc * 257 + SH[dc] + 256], AF.Square,
                           wait=[(f"lx{pp_}", lx), ("sqr", p.val("sqr"))], inc="sqe")
                r_ = p.op("tensor", "matmul", pb[7][0:1, 0:256], ocl[:], sqs[:, 0:256],
                          start=(dc == 0), stop=(dc == 7),
                          wait=[("sqe", sq_)] + ([("lne", vm)] if dc == 0 else []),
                          inc="sqr")
            v1 = p.op("vector", "tensor_scalar", lnr[0:1, 256:512], pb[7][0:1, 0:256], 1.0 / D, None,
                      ALU.mult, wait=[("sqr", r_)], inc="lne")
            v2 = p.op("vector", "tensor_tensor", lnr[0:1, 512:768], lnr[0:1, 0:256], lnr[0:1, 0:256],
                      ALU.mult, wait=[("lne", v1)], inc="lne")
            v3 = p.op("vector", "tensor_tensor", lnr[0:1, 256:512], lnr[0:1, 256:512], lnr[0:1, 512:768],
                      ALU.subtract, wait=[("lne", v2)], inc="lne")
            v4 = p.op("vector", "tensor_scalar", lnr[0:1, 256:512], lnr[0:1, 256:512], 1.0, 1e-5,
                      ALU.mult, ALU.add, wait=[("lne", v3)], inc="lne")
            s1 = p.op("scalar", "activation", lnr[0:1, 512:768], lnr[0:1, 256:512], AF.Sqrt,
                      wait=[("lne", v4)], inc="sqe")
            v5 = p.op("vector", "reciprocal", lnr[0:1, 768:1024], lnr[0:1, 512:768],
                      wait=[("sqe", s1)], inc="lne")
            v6 = p.op("vector", "tensor_tensor", lnr[0:1, 1024:1280], lnr[0:1, 0:256], lnr[0:1, 768:1024],
                      ALU.mult, wait=[("lne", v5)], inc="lne")
            v7 = p.op("vector", "tensor_scalar", lnr[0:1, 1024:1280], lnr[0:1, 1024:1280], -1.0, None,
                      ALU.mult, wait=[("lne", v6)], inc="lne")
            ta = p.op("tensor", "matmul", pb[6][:, 0:256], ons[:], lnr[0:1, 768:1024],
                      start=True, stop=True, wait=[("lne", v7), ("hne", p.val("hne"))], inc="tp")
            tb_ = p.op("tensor", "matmul", pb[7][:, 0:256], ons[:], lnr[0:1, 1024:1280],
                       start=True, stop=True, inc="tp")
            for dc in range(8):
                vh1 = p.op("vector", "scalar_tensor_tensor", sqs[:, 0:256],
                           xf[pp_][:, dc * 257 + SH[dc]:dc * 257 + SH[dc] + 256],
                           lng[:, dc:dc + 1], pb[6][:, 0:256], ALU.mult, ALU.mult,
                           wait=[("tp", tb_), ("hne", p.val("hne"))], inc="hne")
                vh2 = p.op("vector", "tensor_scalar", xst[0][:, 0:256], pb[7][:, 0:256],
                           lng[:, dc:dc + 1], lnb[:, dc:dc + 1], ALU.mult, ALU.add,
                           wait=[("hne", vh1)], inc="hne")
                p.op("vector", "tensor_tensor", hn[:, dc * 256:(dc + 1) * 256], sqs[:, 0:256],
                     xst[0][:, 0:256], ALU.add, wait=[("hne", vh2)], inc="hne")
            HNE = p.val("hne")
            for hc in range(22):
                ab_ = hc % 2
                for dc in range(8):
                    ra = p.op("tensor", "matmul", pb[ab_][:, 0:256],
                              w1a[:, dc * 2816 + hc * 128:dc * 2816 + (hc + 1) * 128],
                              hn[:, dc * 256:(dc + 1) * 256],
                              start=(dc == 0), stop=(dc == 7),
                              wait=([("hne", HNE), ("wldg", LDG), ("mce", max(0, p.val("mce") - 1)),
                                     ("xoe", p.val("xoe"))] if dc == 0 else []),
                              inc="tp" if dc == 7 else None)
                for dc in range(8):
                    rg = p.op("tensor", "matmul", pb[2 + ab_][:, 0:256],
                              w1g[:, dc * 2816 + hc * 128:dc * 2816 + (hc + 1) * 128],
                              hn[:, dc * 256:(dc + 1) * 256],
                              start=(dc == 0), stop=(dc == 7),
                              wait=([("gge", max(0, p.val("gge") - 1)), ("xoe", p.val("xoe"))]
                                    if dc == 0 else []),
                              inc="tp" if dc == 7 else None)
                sg = p.op("scalar", "activation", gg[:, 0:256], pb[2 + ab_][:, 0:256], AF.Gelu,
                          bias=b1g[:, hc:hc + 1], wait=[("tp", rg), ("mce", p.val("mce"))], inc="gge")
                p.op("vector", "scalar_tensor_tensor", mch[:, hc * 256:(hc + 1) * 256],
                     pb[ab_][:, 0:256], b1a[:, hc:hc + 1], gg[:, 0:256], ALU.add, ALU.mult,
                     wait=[("tp", ra), ("gge", sg)], inc="mce")
            MCE = p.val("mce")
            for dmc in range(8):
                for hc in range(22):
                    r_ = p.op("tensor", "matmul", pb[dmc][:, 0:256],
                              w2[:, hc * D + dmc * 128:hc * D + (dmc + 1) * 128],
                              mch[:, hc * 256:(hc + 1) * 256],
                              start=(hc == 0), stop=(hc == 21),
                              wait=[("mce", MCE), ("xoe", max(0, p.val("xoe") - 6))] if hc == 0 else [],
                              inc="tp" if hc == 21 else None)
                vx = p.op("vector", "scalar_tensor_tensor", xst[dmc % 2][:], pb[dmc][:, 0:256],
                          b2[:, dmc:dmc + 1],
                          xf[pp_][:, dmc * 257 + 1:dmc * 257 + 257],
                          ALU.add, ALU.add,
                          wait=[("tp", r_), ("st", xodma.get(dmc % 2, 0))], inc="xoe")
                xodma[dmc % 2] = p.dma("sync", o_x[b, dmc, :, tg * 256:(tg + 1) * 256],
                                       xst[dmc % 2][:], wait=[("xoe", vx)], inc="st")
    if p.fused and l == 0:
        o_h1c = p.dram("halo1c", (B * 8 * 128, HALO), F32, "Internal")
        stv = p.val("st")
        for b in range(B):
            for dc in range(8):
                p.dma("sync", o_h1c[(b * 8 + dc) * 128:(b * 8 + dc + 1) * 128, :],
                      o_x[b, dc, :, C - HALO:], wait=[("st", stv)] if (b == 0 and dc == 0) else [],
                      inc="st")
    p.wait("sync", ("st", p.val("st")))
    return p


def build_p5(p=None):
    p = p or Prog("p5")
    p.phase("p5_")
    EI, EO = "ExternalInput", "ExternalOutput"
    i_x1 = p.dram("x1tm", (B, C, D), F32, EI)
    i_ret = p.dram("ret", (B, 8, 128, C), BF16, EI)
    i_g2 = p.dram("xint2", (B, 8, 128, C), F32, EI)
    i_stat = p.dram("statall", (NCORE, 4), F32, EI)
    i_oc8 = p.dram("onescol8", (8, 1), F32, EI)
    i_gnw = p.dram("gnwT", (128, 8), F32, EI)
    i_gnb = p.dram("gnbT", (128, 8), F32, EI)
    identf = p.dram("identf", (128, 128), F32, EI)
    identb = p.dram("identb", (128, 128), BF16, EI)
    onesrow = p.dram("onesrow", (1, 128), F32, EI)
    o_y = p.dram("y", (B, C, D), F32, EO)

    ret = p.sbuf("ret_s", (128, B * 8 * C), BF16)
    gnw = p.sbuf("gnw_s", (128, 8), F32)
    gnb = p.sbuf("gnb_s", (128, 8), F32)
    st8 = p.sbuf("st8", (NCORE, 4), F32)
    oc8 = p.sbuf("oc8", (8, 1), F32)
    stc = p.sbuf("stc", (4, 4), F32)
    str_ = p.sbuf("str_", (1, 8), F32)
    idf = p.sbuf("idf", (128, 128), F32)
    idb = p.sbuf("idb", (128, 128), BF16)
    ons = p.sbuf("ons", (1, 128), F32)
    sca = p.sbuf("sca", (128, 16), F32)
    scb = p.sbuf("scb", (128, 16), F32)
    x1t = [p.sbuf(f"x1t{i}", (128, D), F32) for i in range(2)]
    g2t = [p.sbuf(f"g2t{i}", (128, 128), F32) for i in range(2)]
    s1t = p.sbuf("s1t", (128, 128), F32)
    z1 = [p.sbuf(f"z1{i}", (128, 128), BF16) for i in range(2)]
    ztm = [p.sbuf(f"ztm{i}", (128, D), BF16) for i in range(2)]
    yt = [p.sbuf(f"yt{i}", (128, D), F32) for i in range(2)]

    pT = [p.psum(f"pT{i}", (128, 1024), BF16) for i in range(2)]
    pS = p.psum("pS5")

    p.dma("sync", ret.rearrange("p (x t) -> p x t", x=B * 8), i_ret.rearrange("b c p t -> p (b c) t"), inc="ld")
    p.dma("sync", gnw[:], i_gnw, inc="ld")
    p.dma("sync", gnb[:], i_gnb, inc="ld")
    p.dma("sync", st8[:], i_stat, inc="ld")
    p.dma("sync", oc8[:], i_oc8, inc="ld")
    p.dma("sync", idf[:], identf, inc="ld")
    p.dma("sync", idb[:], identb, inc="ld")
    p.dma("sync", ons[:], onesrow, inc="ld")
    LD = p.val("ld")

    NTOT = float(H1 * DH1 * T)
    t0_ = p.op("tensor", "matmul", pS[0:4, 4:5], st8[:], oc8[:], start=True, stop=True,
               wait=[("ld", LD)], inc="tp")
    v1 = p.op("vector", "tensor_scalar", stc[:, 1:2], pS[0:4, 4:5], 1.0 / NTOT, None, ALU.mult,
              wait=[("tp", t0_)], inc="vp")
    v2 = v1
    t1 = p.op("tensor", "transpose", pS[0:1, 0:4], stc[0:4, 1:2], idf[0:4, 0:4],
              wait=[("vp", v2)], inc="tp")
    v3 = p.op("vector", "tensor_copy", str_[0:1, 0:4], pS[0:1, 0:4], wait=[("tp", t1)], inc="vp")
    mn = str_[0:1, 0:4].rearrange("p (a x) -> p a x", a=2)[:, :, 0:1]
    mq_ = str_[0:1, 0:4].rearrange("p (a x) -> p a x", a=2)[:, :, 1:2]
    m2 = str_[0:1, 4:6].rearrange("p (a x) -> p a x", a=2)
    vr = str_[0:1, 6:8].rearrange("p (a x) -> p a x", a=2)
    v4 = p.op("vector", "tensor_tensor", m2, mn, mn, ALU.mult, wait=[("vp", v3)], inc="vp")
    v5 = p.op("vector", "tensor_tensor", vr, mq_, m2, ALU.subtract, wait=[("vp", v4)], inc="vp")
    v6 = p.op("vector", "tensor_scalar", str_[0:1, 6:8], str_[0:1, 6:8], 1.0, 1e-5, ALU.mult, ALU.add,
              wait=[("vp", v5)], inc="vp")
    s1_ = p.op("scalar", "activation", str_[0:1, 6:8], str_[0:1, 6:8], AF.Sqrt, wait=[("vp", v6)], inc="sp")
    v7 = p.op("vector", "reciprocal", str_[0:1, 6:8], str_[0:1, 6:8], wait=[("sp", s1_)], inc="vp")
    mcol = p.sbuf("mcol", (128, 2), F32)
    rcol = p.sbuf("rcol", (128, 2), F32)
    tmpc = p.sbuf("tmpc", (128, 1), F32)
    for b in range(B):
        tb1 = p.op("tensor", "matmul", pS[0:128, 4 + b:5 + b], ons[:], str_[0:1, b * 2:b * 2 + 1],
                   start=True, stop=True, wait=[("vp", v7), ("bce", p.val("bce"))], inc="tp")
        p.op("vector", "tensor_copy", mcol[:, b:b + 1], pS[0:128, 4 + b:5 + b], wait=[("tp", tb1)], inc="bce")
        tb2 = p.op("tensor", "matmul", pS[0:128, 6 + b:7 + b], ons[:], str_[0:1, 6 + b:7 + b],
                   start=True, stop=True, inc="tp")
        p.op("vector", "tensor_copy", rcol[:, b:b + 1], pS[0:128, 6 + b:7 + b], wait=[("tp", p.val("tp"))], inc="bce")
    BCE = p.val("bce")
    for b in range(B):
        for dc in range(8):
            j = b * 8 + dc
            va = p.op("vector", "tensor_scalar", sca[:, j:j + 1], gnw[:, dc:dc + 1], rcol[:, b:b + 1],
                      None, ALU.mult, wait=[("bce", BCE), ("ld", LD), ("sce", p.val("sce"))], inc="sce")
            vb_ = p.op("vector", "scalar_tensor_tensor", tmpc[:], sca[:, j:j + 1], -1.0, mcol[:, b:b + 1],
                       ALU.mult, ALU.mult, wait=[("sce", va)], inc="sce")
            p.op("vector", "tensor_tensor", scb[:, j:j + 1], tmpc[:], gnb[:, dc:dc + 1], ALU.add,
                 wait=[("sce", vb_)], inc="sce")
    SCE = p.val("sce")

    ydma = {}
    for b in range(B):
        for tt in range(8):
            i = b * 8 + tt
            pp_ = i % 2
            lx = p.dma("sync", x1t[pp_][:], i_x1[b, tt * 128:(tt + 1) * 128, :],
                       wait=[("ye", max(0, p.val("ye") - 1))], inc=f"lx{pp_}")
            lg2 = p.dma("sync", g2t[pp_].rearrange("p (c t) -> p c t", c=8) if False else g2t[pp_][:],
                        None) if False else None
            for dc in range(8):
                zb = dc % 2
                lg = p.dma("sync", g2t[zb][:],
                           i_g2[b, dc, :, tt * 128:(tt + 1) * 128],
                           wait=[("sile", max(0, p.val("sile") - 1))], inc=f"lg{zb}")
                sv = p.op("scalar", "activation", z1[zb][:], g2t[zb][:], AF.Silu,
                          wait=[(f"lg{zb}", lg), ("zte", max(0, p.val("zte") - 1))], inc="sile")
                v_ = p.op("vector", "tensor_scalar", s1t[:],
                          ret[:, (b * 8 + dc) * C + tt * 128:(b * 8 + dc) * C + (tt + 1) * 128],
                          sca[:, b * 8 + dc:b * 8 + dc + 1], scb[:, b * 8 + dc:b * 8 + dc + 1],
                          ALU.mult, ALU.add, wait=[("sce", SCE), ("zte", max(0, p.val("zte") - 1))],
                          inc="zte")
                v2_ = p.op("vector", "tensor_tensor", z1[zb][:], z1[zb][:], s1t[:], ALU.mult,
                           wait=[("zte", v_), ("sile", p.val("sile"))], inc="zte")
                tz = p.op("tensor", "transpose", pT[zb][:, 0:128], z1[zb][:], idb[:],
                          wait=[("zte", v2_), ("zme", max(0, p.val("zme") - 1)), ("ld", LD)], inc="tp")
                p.op("vector", "tensor_copy", ztm[pp_][:, dc * 128:(dc + 1) * 128], pT[zb][:, 0:128],
                     wait=[("tp", tz)], inc="zme")
            vy = p.op("vector", "tensor_tensor", yt[pp_][:], x1t[pp_][:], ztm[pp_][:], ALU.add,
                      wait=[(f"lx{pp_}", lx), ("zme", p.val("zme")), ("st", ydma.get(pp_, 0))], inc="ye")
            ydma[pp_] = p.dma("sync", o_y[b, tt * 128:(tt + 1) * 128, :], yt[pp_][:],
                              wait=[("ye", vy)], inc="st")
    p.wait("sync", ("st", p.val("st")))
    return p


def _xpos_tiles(a0):
    DIM_HEAD = 64
    WINDOW = 512
    n = T
    half = np.arange(0, DIM_HEAD, 2, dtype=np.float64)
    inv_freq = 1.0 / 10000.0 ** (half / DIM_HEAD)
    t = np.arange(a0 - 640, a0 + C, dtype=np.float64)
    f32idx = np.concatenate([inv_freq, inv_freq])
    freqs = t[:, None] * f32idx[None, :]
    base = (half + 0.4 * DIM_HEAD) / (1.4 * DIM_HEAD)
    base = np.concatenate([base, base])
    power = (t - n // 2) / float(WINDOW // 2)
    scale = base[None, :] ** power[:, None]
    cos = np.cos(freqs); sin = np.sin(freqs)
    sign = np.where(np.arange(64) < 32, -1.0, 1.0)
    csq = 8.0 * cos * scale
    ssq = 8.0 * sin * scale * sign[None, :]
    csk = cos / scale
    ssk = sin / scale * sign[None, :]
    out = []
    for m in (csq, ssq, csk, ssk):
        m128 = np.concatenate([m, m], axis=1)
        out.append(np.ascontiguousarray(m128.T.astype(np.float32)))
    return out


def _sub_masks(core):
    NEGF = -1e30
    mA = np.zeros((9, 128, 640), np.float32)
    row = np.arange(128)[:, None]
    col = np.arange(640)[None, :]
    band = np.where((col >= row) & (col <= row + 512), 0.0, NEGF)
    for qi in range(9):
        mA[qi] = band
        if core == 0 and 1 <= qi <= 4:
            qt = qi - 1
            neg = col < (512 - 128 * qt)
            mA[qi] = np.where(neg, NEGF, mA[qi])
    mB = np.zeros((9, 5, 128, 128), np.float32)
    for qi in range(9):
        for kbi in range(5):
            mB[qi, kbi] = mA[qi][:, kbi * 128:(kbi + 1) * 128].T
    return mA, mB


def subA_inmaps(l, inputs, xi_outs, xi_key):
    cst = _consts()
    import ml_dtypes
    idb = np.eye(128, dtype=ml_dtypes.bfloat16)
    qs = np.tile(inputs["L1_ffn_q_scale"][l], 2).astype(np.float32)
    ks = np.tile(inputs["L1_ffn_k_scale"][l], 2).astype(np.float32)
    perm = np.arange(128) ^ 32
    H2 = np.zeros((128, 128), np.float32)
    H2[0:64, 0] = 1.0
    H2[64:128, 1] = 1.0
    H2T = np.zeros((128, 128), np.float32)
    H2T[0, 0:64] = 1.0
    H2T[1, 64:128] = 1.0
    common = {
        "wqkv": np.ascontiguousarray(inputs["L1_ffn_wqkv"][l]),
        "wo": np.ascontiguousarray(inputs["L1_ffn_wo"][l]),
        "H2": H2, "H2T": H2T,
        "identf": cst["identf"], "identb": idb, "onesrow": cst["onesrow"],
    }
    maps = []
    for k in range(NCORE):
        m = dict(common)
        xi = xi_outs[k][xi_key]
        m["xn1fm" if l == 0 else "xint1"] = xi
        if k == 0:
            if xi is not None:
                m["xihalo"] = np.zeros((B, 8, 128, HALO), xi.dtype)
            m["hvcol"] = np.zeros((128, 1), np.float32)
        else:
            if xi is not None:
                m["xihalo"] = np.ascontiguousarray(xi_outs[k - 1][xi_key][:, :, :, C - HALO:])
            m["hvcol"] = np.ones((128, 1), np.float32)
        csq, ssq, csk, ssk = _xpos_tiles(k * C)
        m["csq"] = csq * qs[:, None]
        m["ssq"] = ssq * qs[perm][:, None]
        m["csk"] = csk * ks[:, None]
        m["ssk"] = ssk * ks[perm][:, None]
        mA, mB = _sub_masks(k)
        m["maskA"] = mA
        maps.append(m)
    return maps


def subF_inmaps(l, inputs, xm_outs):
    cst = _consts()
    HID = 2730
    w1 = inputs["L1_ffn_w1"][l]
    w2 = inputs["L1_ffn_w2"][l]
    b1 = inputs["L1_ffn_b1"][l]
    w1a = np.zeros((D, 2816), np.float32); w1a[:, :HID] = w1[:, :HID]
    w1g = np.zeros((D, 2816), np.float32); w1g[:, :HID] = w1[:, HID:]
    w2p = np.zeros((2816, D), np.float32); w2p[:HID] = w2
    b1a = np.zeros(2816, np.float32); b1a[:HID] = b1[:HID]
    b1g = np.zeros(2816, np.float32); b1g[:HID] = b1[HID:]
    common = {
        "w1a": w1a, "w1g": w1g, "w2p": w2p,
        "b1aT": np.ascontiguousarray(b1a.reshape(22, 128).T),
        "b1gT": np.ascontiguousarray(b1g.reshape(22, 128).T),
        "b2T": np.ascontiguousarray(inputs["L1_ffn_b2"][l].reshape(8, 128).T),
        "lngT": np.ascontiguousarray(inputs["L1_ffn_ln_g"][l].reshape(8, 128).T),
        "lnbT": np.ascontiguousarray(inputs["L1_ffn_ln_b"][l].reshape(8, 128).T),
        "onescol": np.ones((128, 1), np.float32),
        "onesrow": cst["onesrow"],
    }
    maps = []
    for k in range(NCORE):
        m = dict(common)
        xm = xm_outs[k].get(f"xmid{l}", xm_outs[k].get("xmid"))
        if xm is not None:
            m[f"xmid{l}"] = xm
        maps.append(m)
    return maps


def p5_inmaps(inputs, p2couts, p3aouts, sf1outs):
    cst = _consts()
    import ml_dtypes
    idb = np.eye(128, dtype=ml_dtypes.bfloat16)
    gnw = inputs["L1_attn_gn_w"].astype(np.float32)
    gnb = inputs["L1_attn_gn_b"].astype(np.float32)
    gnwT = np.zeros((128, 8), np.float32)
    gnbT = np.zeros((128, 8), np.float32)
    for dc in range(8):
        gnwT[0:64, dc] = gnw[2 * dc]
        gnwT[64:128, dc] = gnw[2 * dc + 1]
        gnbT[0:64, dc] = gnb[2 * dc]
        gnbT[64:128, dc] = gnb[2 * dc + 1]
    statall = np.ascontiguousarray(np.concatenate([o["statp"] for o in p3aouts], axis=0))
    common = {
        "statall": statall, "gnwT": gnwT, "gnbT": gnbT, "onescol8": np.ones((8, 1), np.float32),
        "identf": cst["identf"], "identb": idb, "onesrow": cst["onesrow"],
    }
    maps = []
    for k in range(NCORE):
        m = dict(common)
        m["x1tm"] = p2couts[k]["x1tm"]
        m["ret"] = p3aouts[k]["ret"]
        m["xint2"] = sf1outs[k].get("xint2", sf1outs[k].get("xout"))
        maps.append(m)
    return maps


def _run(build_fn, maps, tag):
    p = build_fn()
    p.check()
    nc = p.build()
    res = run_bass_kernel_spmd(nc, maps, core_ids=list(range(NCORE)))
    return res.results


def kernel(**inputs):
    if os.environ.get("KMODE", "unfused") == "fused":
        return kernel_fused(**inputs)
    r1 = _run(build_p1, p1_inmaps(inputs), "p1")
    r2a = _run(build_p2a, p2a_inmaps(inputs, r1), "p2a")
    r2b = _run(build_p2b, p2b_inmaps(inputs, r1), "p2b")
    r2c = _run(build_p2c, p2c_inmaps(inputs, r1, r2a, r2b), "p2c")
    r3a = _run(build_p3a, p3a_inmaps(inputs, r2c), "p3a")
    rA0 = _run(lambda: build_subA(0), subA_inmaps(0, inputs, r2c, "xn1fm"), "sA0")
    rF0 = _run(lambda: build_subF(0), subF_inmaps(0, inputs, rA0), "sF0")
    rA1 = _run(lambda: build_subA(1), subA_inmaps(1, inputs, rF0, "xint1"), "sA1")
    rF1 = _run(lambda: build_subF(1), subF_inmaps(1, inputs, rA1), "sF1")
    r5 = _run(build_p5, p5_inmaps(inputs, r2c, r3a, rF1), "p5")
    y = np.concatenate([r["y"] for r in r5], axis=1)
    return y.astype(np.float32)


RG = [list(range(NCORE))]


def emit_gather(p, items):
    if os.environ.get("NOCC"):
        for src_ap, dst_name, dst_shape, dt in items:
            p.dram(dst_name, dst_shape, dt, "Internal")
        return p.val("cc")
    stv = p.val("st")
    for src_ap, dst_name, dst_shape, dt in items:
        dst = p.dram(dst_name, dst_shape, dt, "Internal")
        flat = int(np.prod(dst_shape[1:-1])) if len(dst_shape) > 2 else (dst_shape[0] // NCORE)
        last = dst_shape[-1]
        rows = int(np.prod(dst_shape[:-1]))
        dst2 = dst.rearrange(" ".join(f"d{i}" for i in range(len(dst_shape))) + " -> (" +
                             " ".join(f"d{i}" for i in range(len(dst_shape) - 1)) + f") d{len(dst_shape) - 1}") \
            if len(dst_shape) > 2 else dst
        p.op("gpsimd", "collective_compute", "AllGather", ALU.bypass, RG,
             [src_ap], [dst2], wait=[("st", stv), ("cc", p.val("cc"))], inc="cc", incn=16)
    return p.val("cc")


def build_fused():
    p = Prog("mega", fused=True)
    build_p1(p=p)
    sS = p.shared_drams["S"].rearrange("b c p x -> (b c p) x")
    sL = p.shared_drams["Lrow"]
    emit_gather(p, [(sS, "Sall", (NCORE, B, 2, 128, M0), F32),
                    (sL, "Lall", (NCORE, B), F32)])
    build_p2a(p=p)
    build_p2b(p=p)
    build_p2c(p=p)
    sH0 = p.shared_drams["halo0c"]
    sA = p.shared_drams["Amat"].rearrange("b h d e -> (b h d) e")
    sB = p.shared_drams["Bvec"].rearrange("b h e -> (b h) e")
    emit_gather(p, [(sH0, "halo0g", (NCORE, B, 8, 128, HALO), BF16),
                    (sA, "Aall", (NCORE, B, H1, DH1, DH1), F32),
                    (sB, "Bvall", (NCORE, B, H1, DH1), F32)])
    build_p3a(p=p)
    build_subA(0, p=p)
    build_subF(0, p=p)
    sH1 = p.shared_drams["halo1c"]
    sSt = p.shared_drams["statp"]
    emit_gather(p, [(sH1, "halo1g", (NCORE, B, 8, 128, HALO), F32),
                    (sSt, "statall", (NCORE, 4), F32)])
    build_subA(1, p=p)
    build_subF(1, p=p)
    build_p5(p=p)
    return p


def fused_inmaps(inputs, ei_names):
    phase_maps = [
        ("p1_", p1_inmaps(inputs)),
        ("p2a_", p2a_inmaps_w(inputs)),
        ("p2b_", p2b_inmaps_w(inputs)),
        ("p2c_", p2c_inmaps_w(inputs)),
        ("p3a_", p3a_inmaps_w(inputs)),
        ("sA0_", subA_inmaps_w(0, inputs)),
        ("sF0_", subF_inmaps(0, inputs, [{"xmid": None}] * NCORE)),
        ("sA1_", subA_inmaps_w(1, inputs)),
        ("sF1_", subF_inmaps(1, inputs, [{"xmid": None}] * NCORE)),
        ("p5_", p5_inmaps_w(inputs)),
    ]
    maps = []
    for k in range(NCORE):
        m = {}
        for pre, pm in phase_maps:
            for key, val in pm[k].items():
                if val is None:
                    continue
                if key in SHARED:
                    if key in SHARED_EI:
                        m[key] = val
                    continue
                nm = pre + key
                if nm in ei_names:
                    m[nm] = val
        maps.append(m)
    return maps


def _dummy(shape, dt=np.float32):
    return None


def p2a_inmaps_w(inputs):
    fake = [{"S": None, "Lrow": None, "vfm": None, "qfm": None, "ktm": None,
             "chat": None, "ccol": None} for _ in range(NCORE)]
    cst = _consts()
    mk = np.zeros((4, 128, 512), dtype=np.float32)
    for q in range(4):
        mk[q, :, :q * 128] = NEG
        blk = np.where(np.triu(np.ones((128, 128), dtype=np.float32)) > 0, 0.0, NEG)
        mk[q, :, q * 128:(q + 1) * 128] = blk
    maps = []
    for k in range(NCORE):
        m = {
            "WoT": np.ascontiguousarray(inputs["L0_attn_Wo"].T),
            "l0mask": mk, "identf": cst["identf"], "onesrow": cst["onesrow"],
        }
        mlt = np.zeros((1, NCORE), np.float32)
        mlt[0, :k] = 1.0
        koh = np.zeros((1, NCORE), np.float32)
        if k > 0:
            koh[0, k - 1] = 1.0
        m["masklt"] = mlt
        m["konehot"] = koh
        maps.append(m)
    return maps


def p2b_inmaps_w(inputs):
    return p2b_inmaps(inputs, [{"xnfm": None, "wrows": None}] * NCORE)


def p2c_inmaps_w(inputs):
    return p2c_inmaps(inputs, None, [{"sfm": None}] * NCORE, [{"gs": None}] * NCORE)


def p3a_inmaps_w(inputs):
    fake = [{"Amat": np.zeros((B, H1, DH1, DH1), np.float32),
             "Bvec": np.zeros((B, H1, DH1), np.float32),
             "q1fm": None, "k1fm": None, "v1tm": None, "pptm": None}] * NCORE
    maps = p3a_inmaps(inputs, fake)
    for m in maps:
        m.pop("Aall", None)
        m.pop("Bvall", None)
    return maps


def subA_inmaps_w(l, inputs):
    key = "xn1fm" if l == 0 else "xout"
    fake = [{key: None} for _ in range(NCORE)]
    maps = subA_inmaps(l, inputs, fake, key)
    for m in maps:
        m.pop("xi", None)
        m.pop("xihalo", None)
    return maps


def p5_inmaps_w(inputs):
    fake2c = [{"x1tm": None}] * NCORE
    fake3a = [{"ret": None, "statp": np.zeros((1, 4), np.float32)}] * NCORE
    fakef1 = [{"xout": None}] * NCORE
    maps = p5_inmaps(inputs, fake2c, fake3a, fakef1)
    for m in maps:
        m.pop("statall", None)
    return maps


def kernel_fused(**inputs):
    p = build_fused()
    p.check()
    nc = p.build()
    maps = fused_inmaps(inputs, p.ei_names)
    res = run_bass_kernel_spmd(nc, maps, core_ids=list(range(NCORE)))
    y = np.concatenate([r["p5_y"] for r in res.results], axis=1)
    return y.astype(np.float32)
